# version 03 (final)
# baseline (speedup 1.0000x reference)
import numpy as np
import concourse.bass as bass
import concourse.mybir as mybir
from concourse.bass_utils import run_bass_kernel_spmd
from concourse.ordered_engine_set import OrderedEngineSet

F32 = mybir.dt.float32
BF16 = mybir.dt.bfloat16
I32 = mybir.dt.int32
AF = mybir.ActivationFunctionType
ALU = mybir.AluOpType
ds = bass.ds

D, T, B, NB, DC = 2048, 2048, 4, 2, 16
GLA_H, GLA_DK, GLA_DV, GLA_KD, GLA_VD = 8, 128, 256, 1024, 2048
GLA_CH, GLA_NC, GLA_SC, GLA_GLN, GLA_EPS = 64, 32, 16, 16.0, 1e-5
LN_EPS = 1e-5
FFN_H, FFN_HD, FFN_F, FFN_R, FFN_EPS, FFN_MOD = 16, 128, 8192, 16, 1e-6, 0.1
TB_C, TB_N, TB_LR, TB_ROWS, MH = 128, 16, 0.01, 512, 1024

DBG = []
RACE_DETECT = False
SCAN_TILES = 16
SCAN_PARTS = ('chunks', 'epi', 'wo')
SCAN_SUB = ('qt', 'kv', 'smul', 'oi')
STOP = None



class Sem:
    def __init__(self, h, name=""):
        self.h, self.n, self.name = h, 0, name

    def inc(self, k=1):
        self.n += k
        return self.n


class Prog:
    def __init__(self, nc):
        self.nc = nc
        self.ops = {e: [] for e in ("tensor", "vector", "scalar", "gpsimd", "sync")}
        self.ARENA_COLS = 52992
        self.arena = nc.alloc_sbuf_tensor("arena", [128, self.ARENA_COLS], F32)
        self._off = 0
        self._hi = 0
        self.psum = [nc.alloc_psum_tensor(f"pb{i}", [128, 512], F32) for i in range(8)]
        self._semcnt = 0
        self.live_chains = []
        self._lblcnt = 0

    def lbl(self, name):
        self._lblcnt += 1
        return f"{name}{self._lblcnt}"

    def alloc_f32(self, cols, name="", align=1):
        cols = int(cols)
        if align > 1:
            self._off = (self._off + align - 1) // align * align
        off = self._off
        assert off + cols <= self.ARENA_COLS, f"SBUF overflow {name}: {off}+{cols}"
        self._off += cols
        self._hi = max(self._hi, self._off)
        if not hasattr(self, "allocs"):
            self.allocs = []
        self.allocs.append((name, off, cols))
        return self.arena.ap()[:, off:off + cols]

    def alloc_bf16(self, cols, name=""):
        assert cols % 2 == 0
        return self.alloc_f32(cols // 2, name).bitcast(BF16)

    def mark(self):
        return self._off

    def release(self, m):
        self._off = m

    def sem(self, name=None):
        self._semcnt += 1
        return Sem(self.nc.alloc_semaphore(name or f"s{self._semcnt}"),
                   name or f"s{self._semcnt}")

    def t(self, fn): self.ops["tensor"].append(fn)
    def v(self, fn): self.ops["vector"].append(fn)
    def s(self, fn): self.ops["scalar"].append(fn)
    def g(self, fn): self.ops["gpsimd"].append(fn)
    def y(self, fn): self.ops["sync"].append(fn)

    def barrier(self, name):
        pending = []
        for chn in self.live_chains:
            for sm in chn.sems:
                if sm.n:
                    pending.append((sm.h, sm.n))
        sem = self.sem(f"bar_{name}")
        engines = ("tensor", "vector", "scalar", "gpsimd", "sync")
        for e in engines:
            def f(eng, pending=tuple(pending), s=sem):
                for h, v in pending:
                    eng.wait_ge(h, v)
                eng.sem_inc(s.h, 1)
                eng.wait_ge(s.h, len(engines))
            self.ops[e].append(f)
        sem.n = len(engines)
        return sem

    def build(self):
        with self.nc.Block() as block:
            @block.tensor
            def _(t):
                for fn in self.ops["tensor"]:
                    fn(t)

            @block.vector
            def _(v):
                for fn in self.ops["vector"]:
                    fn(v)

            @block.scalar
            def _(s):
                for fn in self.ops["scalar"]:
                    fn(s)

            @block.gpsimd
            def _(g):
                for fn in self.ops["gpsimd"]:
                    fn(g)

            @block.sync
            def _(y):
                for fn in self.ops["sync"]:
                    fn(y)


class Chain:
    NSLOT = 4
    QUEUES = ("sync", "gpsimd", "scalar")
    ENGS = ("tensor", "vector", "scalar", "gpsimd", "sync")

    def __init__(self, P, name):
        self.P = P
        self.sems = []
        self.ceng = {}
        for e in self.ENGS:
            self.ceng[e] = len(self.sems)
            self.sems.append(P.sem(f"{name}_c{e[:2]}"))
        self.slots = {}
        for q in self.QUEUES:
            self.slots[q] = []
            for i in range(self.NSLOT):
                self.slots[q].append(len(self.sems))
                self.sems.append(P.sem(f"{name}_{q}{i}"))
        self.rr = {q: 0 for q in self.QUEUES}
        P.live_chains.append(self)

    def tok(self):
        return tuple(sm.n for sm in self.sems)

    def join(self, *toks):
        toks = [t for t in toks if t is not None]
        return tuple(max(t[i] for t in toks) for i in range(len(self.sems)))

    def zero(self):
        return tuple(0 for _ in self.sems)

    def step(self, engine, fn, after=None, waits=(), dma=False, inc=None,
             manual=False):
        tgt = self.tok() if after is None else after
        if isinstance(tgt, int):
            raise TypeError("Chain.step after= must be a token tuple or None")
        sems = self.sems
        if dma:
            idx = self.slots[engine][self.rr[engine] % self.NSLOT]
            self.rr[engine] += 1
            prev = sems[idx].n
            amt = inc if inc is not None else 16
        else:
            idx = self.ceng[engine]
            prev = None
            amt = inc if inc is not None else 1

        def wrapped(eng, fn=fn, tgt=tgt, waits=tuple(waits), idx=idx,
                    prev=prev, amt=amt, dma=dma, manual=manual):
            for i, v in enumerate(tgt):
                if v and i != idx:
                    eng.wait_ge(sems[i].h, v)
            if dma and prev:
                eng.wait_ge(sems[idx].h, prev)
            elif (not dma) and tgt[idx]:
                pass
            for ws, wv in waits:
                if isinstance(wv, tuple):
                    for i, v in enumerate(wv):
                        if v:
                            eng.wait_ge(ws.sems[i].h, v)
                else:
                    eng.wait_ge(ws.h if isinstance(ws, Sem) else ws, wv)
            if manual:
                fn(eng, sems[idx].h)
                return
            r = fn(eng)
            if dma:
                assert r is not None, "dma step must return the dma instruction"
                r.then_inc(sems[idx].h, amt)
            elif r is None:
                eng.sem_inc(sems[idx].h, amt)
            else:
                r.then_inc(sems[idx].h, amt)
        self.P.ops[engine].append(wrapped)
        sems[idx].inc(amt)
        return self.tok()


def jbranch(g, rj, uid, emit_side):
    eng = OrderedEngineSet([g.engine])
    g.bass.br_lt(rj, 1, on_true=uid + "_0", on_false=uid + "_1", engines=eng)
    for side in (0, 1):
        with g.bass.body(f"{uid}_{side}", valid_engines=eng):
            emit_side(side)
            g.bass.br(uid + "_j", engines=eng)
    g.bass.switch_bb(uid + "_j")


def pbranch(g, rp, uid, emit_p):
    eng = OrderedEngineSet([g.engine])
    g.bass.br_lt(rp, 1, on_true=uid + "_0", on_false=uid + "_c1", engines=eng)
    with g.bass.body(uid + "_0", valid_engines=eng):
        emit_p(0)
        g.bass.br(uid + "_j", engines=eng)
    with g.bass.body(uid + "_c1", valid_engines=eng):
        g.bass.br_lt(rp, 2, on_true=uid + "_1", on_false=uid + "_c2", engines=eng)
    with g.bass.body(uid + "_1", valid_engines=eng):
        emit_p(1)
        g.bass.br(uid + "_j", engines=eng)
    with g.bass.body(uid + "_c2", valid_engines=eng):
        g.bass.br_lt(rp, 3, on_true=uid + "_2", on_false=uid + "_3", engines=eng)
    with g.bass.body(uid + "_2", valid_engines=eng):
        emit_p(2)
        g.bass.br(uid + "_j", engines=eng)
    with g.bass.body(uid + "_3", valid_engines=eng):
        emit_p(3)
        g.bass.br(uid + "_j", engines=eng)
    g.bass.switch_bb(uid + "_j")


class Ctx:
    pass



def build_program():
    nc = bass.Bass("TRN2", target_bir_lowering=False, debug=False, num_devices=8,
                   detect_race_conditions=RACE_DETECT)
    P = Prog(nc)
    C = Ctx()
    C.nc, C.P = nc, P
    I = {}
    C.I = I

    def inp(name, shape, dtype=F32):
        I[name] = nc.dram_tensor(name, list(shape), dtype, kind="ExternalInput").ap()

    inp("xin", (NB, T, D))
    inp("aWq", (D, GLA_KD)); inp("aWk", (D, GLA_KD))
    inp("aWv", (D, GLA_VD)); inp("aWg", (D, GLA_VD))
    inp("aWgk1", (D, 16)); inp("aWgk2", (16, GLA_KD)); inp("a_bgk2", (1, GLA_KD))
    inp("aWo", (GLA_VD, D))
    inp("a_lnw", (1, GLA_DV)); inp("a_lnb", (1, GLA_DV))
    inp("fWq", (D, D)); inp("fWk", (D, D)); inp("fWv", (D, D)); inp("fWo", (D, D))
    inp("fA", (D, FFN_R)); inp("fB", (FFN_R, FFN_F))
    inp("fWin", (FFN_F, D)); inp("fWup", (FFN_F, D)); inp("fWdown", (D, FFN_F))
    inp("f_ln1", (1, D)); inp("f_ln2", (1, D))
    inp("bW0h", (MH, D)); inp("bW1h", (D, MH))
    inp("b_b0h", (1, MH)); inp("b_b1", (1, D))
    inp("b_lrw", (1, D)); inp("b_lrb", (1, 1))
    inp("b_fgw", (1, D)); inp("b_fgb", (1, 1))
    inp("identf", (128, 128))
    inp("tril64", (128, 128))
    inp("suf16", (128, 128))
    inp("cmask", (128, 128))
    inp("meta", (1, 16), I32)

    C.out = nc.dram_tensor("out", [T, D], F32, kind="ExternalOutput").ap()
    C.out_row0 = 0
    C.dbg = {}

    def dout(name, shape, dtype=F32):
        if name in DBG:
            C.dbg[name] = nc.dram_tensor("dbg_" + name, list(shape), dtype,
                                         kind="ExternalOutput").ap()
            return C.dbg[name]
        return None
    C.dout = dout

    def dram(name, shape, dtype=F32, **kw):
        return nc.dram_tensor(name, list(shape), dtype, **kw)
    C.dram = dram

    C.xT = [dram(f"xT{i}", (D, T)) for i in range(NB)]
    C.gbuf = dram("gbuf", (GLA_VD, T))
    C.qTb = dram("qTb", (D, T), BF16)
    C.kTb = dram("kTb", (D, T), BF16)
    C.vtb = dram("vtb", (T, D), BF16)
    C.vtb2 = dram("vtb2", (D, T), BF16)
    C.hbuf = dram("hbuf", (D, T), BF16)
    C.ubuf = dram("ubuf", (D, T))
    C.mlpacc = dram("mlpacc", (D, T))
    C.x2all = dram("x2all", (B, D, T), addr_space="Shared")
    C.shexch = dram("shexch", (2, 2, D, TB_ROWS), BF16, addr_space="Shared")
    C.shex2 = dram("shex2", (2, D, T), addr_space="Shared")
    C.shflag = dram("shflag", (2, 16), I32, addr_space="Shared")
    C.x3loc = dram("x3loc", (B, D, T))
    C.w1tb = dram("w1tb", (MH, D), BF16)
    C.x3tok = dram("x3tok", (B, T, D), BF16)
    C.xffn2d = dram("xffn2", (D, T))
    C.aoutp = dram("aoutp", (D, T))

    C.stgTdims = {"fWq": (D, D), "fWk": (D, D), "fWv": (D, D), "fWo": (D, D),
                  "fWin": (FFN_F, D), "fWup": (FFN_F, D), "fWdown": (D, FFN_F)}
    C.stgT = {w: dram(w + "_t", (o // 128, i // 128, 128, 128), BF16)
              for w, (o, i) in C.stgTdims.items()}
    C.stgC = {w: dram(w + "_c", shp, BF16) for w, shp in C.stgTdims.items()}
    C.stg = {"fB": dram("fB_b", (FFN_R, FFN_F), BF16)}

    P.chain = Chain(P, "M")
    C.identf = P.alloc_f32(128, "identf")
    C.identb = P.alloc_bf16(128, "identb")
    C.tril64 = P.alloc_bf16(128, "tril64")
    C.suf16 = P.alloc_bf16(128, "suf16")
    C.cmask = P.alloc_f32(128, "cmask")
    C.onescolf = P.alloc_f32(1, "onescolf")
    C.onescolb = P.alloc_bf16(2, "onescolb")
    C.metat = nc.alloc_sbuf_tensor("metat", [1, 16], I32).ap()
    C.flagtile = nc.alloc_sbuf_tensor("flagt", [1, 16], I32).ap()
    C.x3tr = P.alloc_bf16(128, "x3tr")
    C.pollsem = P.sem("poll")
    C.pollctr = [0]
    col_specs = [("f_ln1", D), ("f_ln2", D), ("b_b0h", MH), ("b_b1", D),
                 ("b_lrw", D), ("b_fgw", D), ("b_lrb", 1), ("b_fgb", 1)]
    C.col_specs = col_specs
    total = sum((n + 127) // 128 for _, n in col_specs)
    area = P.alloc_f32(total, "veccols")
    C.cols = {}
    _o = 0
    for name, n in col_specs:
        k = (n + 127) // 128
        C.cols[name] = area[:, _o:_o + k]
        _o += k
    C.colsb = {"b_lrw": P.alloc_bf16(16, "lrwb"), "b_fgw": P.alloc_bf16(16, "fgwb")}
    C.gla_mark = P.mark()
    C.bgk2rep = P.alloc_f32(GLA_KD, "bgk2rep")
    C.alnwrep = P.alloc_f32(GLA_VD, "alnwrep")
    C.alnbrep = P.alloc_f32(GLA_VD, "alnbrep")
    C.rows_mark = P.mark()
    C.rowsbig = P.alloc_f32(14336, "rowsbig")[0:1, :]
    C.rowsbig2 = P.alloc_f32(1024, "rowsbig2")[0:1, :]

    ch = P.chain
    ch.step("sync", lambda y: y.dma_start(out=C.identf, in_=I["identf"]), after=ch.zero(), dma=True)
    ch.step("gpsimd", lambda g: g.dma_start(out=C.identb, in_=I["identf"]), after=ch.zero(), dma=True)
    ch.step("gpsimd", lambda g: g.dma_start(out=C.tril64, in_=I["tril64"]), after=ch.zero(), dma=True)
    ch.step("gpsimd", lambda g: g.dma_start(out=C.suf16, in_=I["suf16"]), after=ch.zero(), dma=True)
    ch.step("sync", lambda y: y.dma_start(out=C.cmask, in_=I["cmask"]), after=ch.zero(), dma=True)
    mtk = ch.step("sync", lambda y: y.dma_start(out=C.metat[0:1, 0:16], in_=I["meta"]), after=ch.zero(), dma=True)

    def f_pin(g):
        C.rj = g.alloc_register("rj")
        C.rp = g.alloc_register("rp")
        C.rpw = g.alloc_register("rpw")
        g.reg_load(C.rj, C.metat[0:1, 0:1])
        g.reg_load(C.rp, C.metat[0:1, 2:3])
        g.reg_mov(C.rpw, 0)
        return None
    ch.step("gpsimd", f_pin, after=mtk)

    def memsets(v):
        v.memset(C.onescolf[:, 0:1], 1.0)
        v.memset(C.rowsbig[0:1, 12288:14336], 0.0)
        return v.memset(C.onescolb[:, 0:1], 1.0)
    mtok = ch.step("vector", memsets, after=ch.zero())

    ROWS = [("a_bgk2", GLA_KD), ("f_ln1", D), ("f_ln2", D), ("b_b0h", MH), ("b_b1", D),
            ("b_lrw", D), ("b_fgw", D), ("b_lrb", 1), ("b_fgb", 1)]
    C.rowoff = {}
    off = 0
    for name, n in ROWS:
        C.rowoff[name] = off
        ch.step("sync", lambda y, name=name, off=off, n=n:
                y.dma_start(out=C.rowsbig[0:1, off:off + n], in_=I[name]), after=mtok, dma=True)
        off += n
    C.rowoff2 = {}
    off2 = 0
    for name, n in (("a_lnw", GLA_DV), ("a_lnb", GLA_DV)):
        C.rowoff2[name] = off2
        ch.step("sync", lambda y, name=name, off2=off2, n=n:
                y.dma_start(out=C.rowsbig2[0:1, off2:off2 + n], in_=I[name]), after=ch.zero(), dma=True)
        off2 += n

    P.barrier("boot")

    mk_cols(C)
    P.barrier("cols")
    emit_staging(C)
    if C.dout("stg_c0", (128, D), BF16) is not None:
        P.chain.step("sync", lambda y: y.dma_start(
            out=C.dbg["stg_c0"], in_=C.stgC["fWq"][0:128, :]),
            waits=[C.stg_done["fWq"]], dma=True)
    if C.dout("stg_t0", (16 * 128, 128), BF16) is not None:
        P.chain.step("sync", lambda y: y.dma_start(
            out=C.dbg["stg_t0"].rearrange("(c p) n -> c p n", p=128),
            in_=C.stgT["fWq"][0]),
            waits=[C.stg_done["fWq"]], dma=True)

    for bi in range(NB):
        emit_input_transpose(C, bi)
    P.barrier("itr")
    if STOP == "input":
        finalize_dummy(C)
        return nc, P, C

    for bi in range(NB):
        emit_attn_a(C, bi)
        P.barrier(f"attn_a{bi}")
        if STOP in ("attn_a0", "gla_ln", "gla_proj", "gla_pre") and bi == 0:
            finalize_dummy(C)
            return nc, P, C
    if STOP == "attn_a":
        finalize_dummy(C)
        return nc, P, C

    for bi in range(NB):
        emit_ffn(C, bi)
        P.barrier(f"ffn{bi}")
        if STOP is not None and bi == 0 and (STOP == "ffn0" or STOP.startswith("f0_")):
            finalize_dummy(C)
            return nc, P, C
    if STOP == "ffn":
        finalize_dummy(C)
        return nc, P, C

    emit_exchange(C)
    P.barrier("exch")
    if STOP == "exchange":
        finalize_dummy(C)
        return nc, P, C

    emit_attn_b(C)
    P.barrier("attn_b")
    if STOP == "attn_b":
        finalize_dummy(C)
        return nc, P, C

    emit_ffn(C, None, src_dram=C.xffn2d, tag="f2", final=True)
    P.barrier("ffn2")
    return nc, P, C


def finalize_dummy(C):
    P = C.P
    ch = P.chain
    z = C.identf
    zz = P.alloc_f32(16, "zfin")
    ch.step("vector", lambda v: v.memset(zz[:, :], 0.0), after=ch.zero())
    P.barrier("fin0")
    for r in range(0, T // 2, 128):
        ch.step("sync", lambda y, r=r: y.dma_start(out=C.out[r:r + 128, 0:16], in_=zz[:, 0:16]), dma=True)
    C.P.barrier("end")



def mk_cols(C):
    P = C.P
    ps = P.psum
    ch = P.chain
    col_specs = C.col_specs

    def f_t(t):
        r = None
        o = 0
        for name, n in col_specs:
            k = (n + 127) // 128
            ro = C.rowoff[name]
            for j in range(k):
                r = t.matmul(ps[0].ap()[0:128, o + j: o + j + 1],
                             C.rowsbig[0:1, ro + j * 128: ro + j * 128 + 128],
                             C.onescolf[0:1, 0:1],
                             start=(o + j == 0), stop=False, skip_group_check=True)
            o += k
        return r
    ch.step("tensor", f_t, after=ch.zero())

    def f_v(v):
        o = 0
        r = None
        for name, n in col_specs:
            k = (n + 127) // 128
            r = v.tensor_copy(C.cols[name], ps[0].ap()[:, o:o + k])
            o += k
        return r
    ch.step("vector", f_v)

    def f_v2(v):
        v.tensor_copy(C.colsb["b_lrw"], C.cols["b_lrw"])
        return v.tensor_copy(C.colsb["b_fgw"], C.cols["b_fgw"])
    ch.step("vector", f_v2)

    ones128 = C.onescolf[0:1, 0:1].broadcast_to([1, 128])

    def rep_row(ch, src_slicer, dst, n):
        nb = (n + 511) // 512
        for blk in range(nb):
            w = min(512, n - blk * 512)

            def f_t(t, blk=blk, w=w):
                return t.matmul(ps[blk % 4].ap()[:, 0:w], ones128,
                                src_slicer(blk, w), start=True, stop=True)
            tok = ch.step("tensor", f_t)

            def f_v(v, blk=blk, w=w):
                return v.tensor_copy(dst[:, blk * 512: blk * 512 + w],
                                     ps[blk % 4].ap()[:, 0:w])
            ch.step("vector", f_v, after=tok)

    ro = C.rowoff["a_bgk2"]
    rep_row(ch, lambda blk, w, ro=ro: C.rowsbig[0:1, ro + blk * 512: ro + blk * 512 + w],
            C.bgk2rep, GLA_KD)
    for src, dstname in (("a_lnw", "alnwrep"), ("a_lnb", "alnbrep")):
        ro2 = C.rowoff2[src]
        r3 = C.rowsbig2[0:1, ro2:ro2 + GLA_DV].unsqueeze(1).broadcast_to([1, 8, GLA_DV])
        dst = getattr(C, dstname)
        rep_row(ch, lambda blk, w, r3=r3: r3[0:1, 2 * blk:2 * blk + 2, :],
                dst, GLA_VD)
    C.rep_row = rep_row
    C.ones128 = ones128
    P.release(C.rows_mark)



def emit_staging(C):
    P = C.P
    chc = Chain(P, "stgc")
    cht = Chain(P, "stgt")
    C.stg_mark = P.mark()
    bufA = P.alloc_bf16(4096, "stgbufA")
    bufB = P.alloc_bf16(4096, "stgbufB")
    bufC = P.alloc_f32(2048, "stgbufC", align=8).bitcast(BF16)
    bufD = P.alloc_f32(2048, "stgbufD", align=8).bitcast(BF16)

    order = ["fWq", "fWk", "fWv", "fWin", "fWup", "fWo", "fWdown"]
    cast_jobs = []
    for w in order:
        Rr, Cc = C.stgTdims[w]
        for r in range(0, Rr, 128):
            for c0 in range(0, Cc, 4096):
                cast_jobs.append((w, r, c0, min(4096, Cc - c0)))
    for c0 in range(0, FFN_F, 4096):
        cast_jobs.append(("fB", 0, c0, 4096))

    store_tok = {}
    for idx, (w, r, c0, cw) in enumerate(cast_jobs):
        buf = bufA if idx % 2 == 0 else bufB
        waits = []
        if idx >= 2:
            waits = [(chc, store_tok[idx - 2])]

        def f_g(g, w=w, r=r, c0=c0, cw=cw, buf=buf):
            if w == "fB":
                return g.dma_start(out=buf[0:FFN_R, 0:cw], in_=C.I["fB"][0:FFN_R, c0:c0 + cw])
            return g.dma_start(out=buf[:, 0:cw], in_=C.I[w][r:r + 128, c0:c0 + cw])
        ld = chc.step("gpsimd", f_g, after=chc.zero(), waits=waits, dma=True)

        def f_y(y, w=w, r=r, c0=c0, cw=cw, buf=buf):
            if w == "fB":
                return y.dma_start(out=C.stg["fB"][0:FFN_R, c0:c0 + cw], in_=buf[0:FFN_R, 0:cw])
            return y.dma_start(out=C.stgC[w][r:r + 128, c0:c0 + cw], in_=buf[:, 0:cw])
        store_tok[idx] = chc.step("sync", f_y, after=ld, dma=True)

    cast_done = {}
    for idx, (w, r, c0, cw) in enumerate(cast_jobs):
        cast_done[w] = store_tok[idx]
    C.cast_done = {w: (chc, v) for w, v in cast_done.items()}

    tp_jobs = []
    for w in order:
        Rr, Cc = C.stgTdims[w]
        for ic in range(Cc // 128):
            for r0 in range(0, Rr, 4096):
                tp_jobs.append((w, ic, r0, min(4096, Rr - r0)))

    tstore_tok = {}
    for idx, (w, ic, r0, rr) in enumerate(tp_jobs):
        buf = bufC if idx % 2 == 0 else bufD
        waits = [(chc, cast_done[w])]
        if idx >= 2:
            waits.append((cht, tstore_tok[idx - 2]))

        def f_s(s, w=w, ic=ic, r0=r0, rr=rr, buf=buf):
            return s.dma_start(out=buf[:, 0:rr],
                               in_=C.stgC[w][r0:r0 + rr, ic * 128:(ic + 1) * 128],
                               transpose=True)
        ld = cht.step("scalar", f_s, after=cht.zero(), waits=waits, dma=True)

        def f_y(y, w=w, ic=ic, r0=r0, rr=rr, buf=buf):
            dst = C.stgT[w][r0 // 128: (r0 + rr) // 128, ic, 0:128, 0:128].rearrange(
                "oc p q -> p oc q")
            src = buf[:, 0:rr].rearrange("p (oc q) -> p oc q", q=128)
            return y.dma_start(out=dst, in_=src)
        tstore_tok[idx] = cht.step("sync", f_y, after=ld, dma=True)

    stg_done = {}
    for idx, (w, ic, r0, rr) in enumerate(tp_jobs):
        stg_done[w] = tstore_tok[idx]
    C.stg_done = {w: (cht, v) for w, v in stg_done.items()}
    C.stg_done["fB"] = C.cast_done["fB"]



def emit_input_transpose(C, bi):
    P = C.P
    ch = P.chain
    m = P.mark()
    src = [P.alloc_f32(2048, "itA"), P.alloc_f32(2048, "itB")]
    dst = [P.alloc_f32(2048, "itC"), P.alloc_f32(2048, "itD")]
    st_tok = {}
    for tt in range(16):
        sbuf, dbuf = src[tt % 2], dst[tt % 2]
        waits = []
        if tt >= 2:
            waits.append((ch, st_tok[tt - 2]))
        ld = ch.step("sync", lambda y, tt=tt, b=sbuf:
                     y.dma_start(out=b[:, 0:2048], in_=C.I["xin"][bi, tt * 128:(tt + 1) * 128, 0:D]),
                     after=ch.zero(), waits=waits, dma=True)
        prev_ev = None
        for c in range(16):
            bank = P.psum[4 + c % 2]
            waits2 = []
            mm = ch.step("tensor", lambda t, b=sbuf, c=c, bank=bank:
                         t.transpose(bank.ap()[0:128, 0:128], b[:, c * 128:(c + 1) * 128], C.identf),
                         after=ld if c == 0 else None, waits=waits2)
            ev = ch.step("vector", lambda v, d=dbuf, c=c, bank=bank:
                         v.tensor_copy(d[:, c * 128:(c + 1) * 128], bank.ap()[0:128, 0:128]),
                         after=mm)
            st_tok[(tt, c)] = ev
        st_tok[tt] = ch.step("sync", lambda y, tt=tt, d=dbuf:
                             y.dma_start(out=C.xT[bi].ap().rearrange("(c p) t -> p c t", p=128)[:, :, tt * 128:(tt + 1) * 128],
                                         in_=d[:, 0:2048].rearrange("p (c q) -> p c q", q=128)),
                             dma=True)
    P.release(m)



def emit_ln_band(C, ch, src_dram, tb, outs, name="ln", loader=None):
    P = C.P
    ps = P.psum
    m = P.mark()
    xb = P.alloc_f32(16 * 512, name + "x")
    sq = P.alloc_f32(512, name + "sq")
    murep = P.alloc_f32(512, name + "mu")
    srep = P.alloc_f32(512, name + "s")
    need_srms = "sc2" in outs
    srms = P.alloc_f32(512, name + "srms") if need_srms else None

    if loader is None:
        src = src_dram if isinstance(src_dram, bass.AP) else src_dram.ap()
        ld = ch.step("sync", lambda y: y.dma_start(
            out=xb[:, 0:16 * 512].rearrange("p (c t) -> p c t", t=512),
            in_=src.rearrange("(c p) t -> p c t", p=128)[:, :, tb * 512:(tb + 1) * 512]),
            dma=True)
    else:
        ld = ch.step("gpsimd", lambda y: loader(
            y, tb, xb[:, 0:16 * 512].rearrange("p (c t) -> p c t", t=512)), dma=True)

    def f_stats(t):
        r = None
        for c in range(DC):
            r = t.matmul(ps[0].ap()[0:1, 0:512], C.onescolf[:, 0:1],
                         xb[:, c * 512:(c + 1) * 512], start=(c == 0), stop=(c == DC - 1))
        return r
    ch.step("tensor", f_stats, after=ld)

    prev = ld
    for c in range(DC):
        s1 = ch.step("scalar", lambda s, c=c: s.activation(
            sq[:, 0:512], xb[:, c * 512:(c + 1) * 512], AF.Square), after=prev)
        prev = ch.step("tensor", lambda t, c=c: t.matmul(
            ps[1].ap()[0:1, 0:512], C.onescolf[:, 0:1], sq[:, 0:512],
            start=(c == 0), stop=(c == DC - 1)), after=s1)

    def f_rows(v):
        v.tensor_scalar_mul(murep[0:1, 0:512], ps[0].ap()[0:1, 0:512], 1.0 / D)
        v.tensor_scalar_mul(srep[0:1, 0:512], ps[1].ap()[0:1, 0:512], 1.0 / D)
        v.tensor_mul(sq[0:1, 0:512], murep[0:1, 0:512], murep[0:1, 0:512])
        v.tensor_sub(srep[0:1, 0:512], srep[0:1, 0:512], sq[0:1, 0:512])
        if need_srms:
            v.tensor_scalar_add(sq[0:1, 0:512], srep[0:1, 0:512], LN_EPS)
            v.reciprocal(srms[0:1, 0:512], sq[0:1, 0:512])
            v.tensor_mul(srms[0:1, 0:512], srms[0:1, 0:512], srep[0:1, 0:512])
            v.tensor_scalar_add(srms[0:1, 0:512], srms[0:1, 0:512], FFN_EPS)
        return v.tensor_scalar_add(srep[0:1, 0:512], srep[0:1, 0:512], LN_EPS)
    t0 = ch.step("vector", f_rows)

    def f_sqrt(s):
        s.activation(srep[0:1, 0:512], srep[0:1, 0:512], AF.Sqrt)
        if need_srms:
            return s.activation(srms[0:1, 0:512], srms[0:1, 0:512], AF.Sqrt)
        return None
    t1 = ch.step("scalar", f_sqrt, after=t0)

    def f_rcp(v):
        v.reciprocal(srep[0:1, 0:512], srep[0:1, 0:512])
        if need_srms:
            return v.reciprocal(srms[0:1, 0:512], srms[0:1, 0:512])
        return None
    t2 = ch.step("vector", f_rcp, after=t1)

    def f_rep(t):
        t.matmul(ps[2].ap()[:, 0:512], C.onescolf[0:1, 0:1].broadcast_to([1, 128]),
                 murep[0:1, 0:512], start=True, stop=True)
        r = t.matmul(ps[3].ap()[:, 0:512], C.onescolf[0:1, 0:1].broadcast_to([1, 128]),
                     srep[0:1, 0:512], start=True, stop=True)
        if need_srms:
            r = t.matmul(ps[4].ap()[:, 0:512], C.onescolf[0:1, 0:1].broadcast_to([1, 128]),
                         srms[0:1, 0:512], start=True, stop=True)
        return r
    t3 = ch.step("tensor", f_rep, after=t2)

    def f_cprep(v):
        v.tensor_copy(murep[:, 0:512], ps[2].ap()[:, 0:512])
        r = v.tensor_copy(srep[:, 0:512], ps[3].ap()[:, 0:512])
        if need_srms:
            r = v.tensor_copy(srms[:, 0:512], ps[4].ap()[:, 0:512])
        return r
    t4 = ch.step("vector", f_cprep, after=t3)

    toks = []
    for c in range(DC):
        guard = toks[c - 2] if c >= 2 else None

        def f_norm(v, c=c):
            x = xb[:, c * 512:(c + 1) * 512]
            v.tensor_sub(x, x, murep[:, 0:512])
            if "u" in outs:
                r = v.tensor_mul(outs["u"](c), x, srep[:, 0:512])
            if "xb" in outs:
                v.tensor_mul(x, x, srep[:, 0:512])
                r = v.tensor_copy(outs["xb"](c), x)
            if "sc2" in outs:
                r = v.tensor_mul(x, outs["u"](c), srms[:, 0:512])
            return r
        tk = ch.step("vector", f_norm,
                     after=t4 if guard is None else ch.join(t4, guard))
        if "sc2" in outs:
            colw1, fn1, colw2, fn2 = outs["sc2"]

            def f_sc(s, c=c):
                s.activation(fn1(c), xb[:, c * 512:(c + 1) * 512], AF.Copy,
                             scale=colw1[:, c:c + 1])
                return s.activation(fn2(c), xb[:, c * 512:(c + 1) * 512], AF.Copy,
                                    scale=colw2[:, c:c + 1])
            tk = ch.step("scalar", f_sc, after=tk)
        if "post" in outs:
            tk = outs["post"](ch, c, tk)
        toks.append(tk)
    P.release(m)
    return toks



def emit_attn_a(C, bi):
    P = C.P
    ps = P.psum
    ch = P.chain
    m0 = P.mark()

    t1T = P.alloc_bf16(2048, "gla_t1")
    gk2b = P.alloc_bf16(1024, "gla_gk2")
    gk1b = P.alloc_bf16(16 * 16, "gla_gk1")
    colsum = P.alloc_f32(16 * 8, "gla_cs")
    blrows = P.alloc_f32(1024, "gla_blr")
    eblT = P.alloc_f32(8 * 32, "gla_ebl")
    S = P.alloc_f32(8 * 256, "gla_S")

    if not hasattr(C, "qatok"):
        C.qatok = C.dram("qatok", (T, GLA_KD), BF16)
        C.katok = C.dram("katok", (T, GLA_KD), BF16)
        C.b0tok = C.dram("b0tok", (T, GLA_KD), BF16)

    m1 = P.mark()
    xt = P.alloc_bf16(16 * 2048, "gla_xt")
    for tb in range(4):
        emit_ln_band(C, ch, C.xT[bi], tb,
                     {"xb": (lambda c, tb=tb: xt[:, c * 2048 + tb * 512: c * 2048 + (tb + 1) * 512])},
                     name=f"glaln{bi}_{tb}")
    if C.dout(f"gla_xt{bi}", (D, T), BF16) is not None:
        dst = C.dbg[f"gla_xt{bi}"]
        ch.step("sync", lambda y, dst=dst: y.dma_start(
            out=dst.rearrange("(c p) t -> p c t", p=128),
            in_=xt[:, :].rearrange("p (c t) -> p c t", t=2048)), dma=True)
    if STOP == "gla_ln":
        P.release(m0)
        return
    wq = P.alloc_bf16(16 * 512, "gla_w")
    vtile = P.alloc_f32(512, "gla_vt")

    ch.step("gpsimd", lambda g: g.dma_start(
        out=gk1b[:, :].rearrange("p (c r) -> p c r", r=16),
        in_=C.I["aWgk1"].rearrange("(c p) r -> p c r", p=128),
        allow_slow_non_contiguous=True), after=ch.zero(), dma=True)
    ch.step("gpsimd", lambda g: g.dma_start(out=gk2b[0:16, 0:1024], in_=C.I["aWgk2"]),
            after=ch.zero(), dma=True)

    for tbl in range(4):
        def f_t(t, tbl=tbl):
            r = None
            for c in range(DC):
                r = t.matmul(ps[5].ap()[0:16, 0:512], gk1b[:, c * 16:(c + 1) * 16],
                             xt[:, c * 2048 + tbl * 512: c * 2048 + (tbl + 1) * 512],
                             start=(c == 0), stop=(c == DC - 1))
            return r
        tk = ch.step("tensor", f_t)
        ch.step("vector", lambda v, tbl=tbl: v.tensor_copy(
            t1T[0:16, tbl * 512:(tbl + 1) * 512], ps[5].ap()[0:16, 0:512]), after=tk)

    def proj(Wname, NCOLS, out_dram, out_f32_dram=None):
        for nb in range(NCOLS // 512):
            ch.step("gpsimd", lambda g, nb=nb, Wname=Wname: g.dma_start(
                out=wq[:, :].rearrange("p (c n) -> p c n", n=512),
                in_=C.I[Wname].rearrange("(c p) n -> p c n", p=128)[:, :, nb * 512:(nb + 1) * 512]),
                dma=True)
            for tt in range(16):
                bank = ps[tt % 2]

                def f_t(t, tt=tt, bank=bank):
                    r = None
                    for c in range(DC):
                        r = t.matmul(bank.ap()[:, 0:512],
                                     xt[:, c * 2048 + tt * 128: c * 2048 + tt * 128 + 128],
                                     wq[:, c * 512:(c + 1) * 512],
                                     start=(c == 0), stop=(c == DC - 1))
                    return r
                tk = ch.step("tensor", f_t)
                if out_f32_dram is not None:
                    ev = ch.step("vector", lambda v, bank=bank: v.tensor_copy(
                        vtile[:, 0:512], bank.ap()[:, 0:512]), after=tk)
                    ch.step("sync", lambda y, tt=tt, nb=nb, d=out_f32_dram: y.dma_start(
                        out=d[tt * 128:(tt + 1) * 128, nb * 512:(nb + 1) * 512],
                        in_=vtile[:, 0:512]), after=ev, dma=True)
                else:
                    ev = ch.step("vector", lambda v, bank=bank: v.tensor_copy(
                        vtile[:, 0:512].bitcast(BF16)[:, 0:512], bank.ap()[:, 0:512]),
                        after=tk)
                    ch.step("sync", lambda y, tt=tt, nb=nb, d=out_dram: y.dma_start(
                        out=d[tt * 128:(tt + 1) * 128, nb * 512:(nb + 1) * 512],
                        in_=vtile[:, 0:512].bitcast(BF16)[:, 0:512]), after=ev, dma=True)

    proj("aWq", GLA_KD, C.qatok)
    proj("aWk", GLA_KD, C.katok)
    proj("aWv", GLA_VD, C.vtb)
    proj("aWg", GLA_VD, None, out_f32_dram=C.gbuf)

    for nm, dbuf, wid in (("gla_q", C.qatok, GLA_KD), ("gla_k", C.katok, GLA_KD)):
        if C.dout(f"{nm}{bi}", (T, wid), BF16) is not None:
            dst = C.dbg[f"{nm}{bi}"]
            ch.step("sync", lambda y, dst=dst, dbuf=dbuf: y.dma_start(
                out=dst, in_=dbuf.ap() if hasattr(dbuf, "ap") else dbuf), dma=True)
    P.release(m1)
    if STOP == "gla_proj":
        P.release(m0)
        return

    m2 = P.mark()
    qtile = P.alloc_bf16(1024, "gla_qt")
    ktile = P.alloc_bf16(1024, "gla_kt")
    b0tile = P.alloc_bf16(1024, "gla_b0t")
    gkp = P.alloc_f32(1024, "gla_gkp")
    tmp1 = P.alloc_f32(1024, "gla_tm1")
    tmp2 = P.alloc_f32(1024, "gla_tm2")
    gkb = P.alloc_bf16(1024, "gla_gkb")
    ebc = P.alloc_f32(1024, "gla_ebc")
    embc = P.alloc_f32(1024, "gla_embc")
    atile = P.alloc_bf16(1024, "gla_at")
    mtile = P.alloc_f32(1024, "gla_mt")
    rowt = P.alloc_f32(1024, "gla_rowt")

    for tt in range(16):
        ldq = ch.step("sync", lambda y, tt=tt: y.dma_start(
            out=qtile[:, :], in_=C.qatok[tt * 128:(tt + 1) * 128, :]), dma=True)
        ldk = ch.step("sync", lambda y, tt=tt: y.dma_start(
            out=ktile[:, :], in_=C.katok[tt * 128:(tt + 1) * 128, :]), dma=True)

        def f_t(t, tt=tt):
            t.matmul(ps[0].ap()[:, 0:512], t1T[0:16, tt * 128:(tt + 1) * 128],
                     gk2b[0:16, 0:512], start=True, stop=True)
            return t.matmul(ps[1].ap()[:, 0:512], t1T[0:16, tt * 128:(tt + 1) * 128],
                            gk2b[0:16, 512:1024], start=True, stop=True)
        tk = ch.step("tensor", f_t)

        def f_v(v):
            v.tensor_add(gkp[:, 0:512], ps[0].ap()[:, 0:512], C.bgk2rep[:, 0:512])
            return v.tensor_add(gkp[:, 512:1024], ps[1].ap()[:, 0:512], C.bgk2rep[:, 512:1024])
        tk = ch.step("vector", f_v, after=tk)

        def f_s(s):
            s.activation(tmp1[:, 0:1024], gkp[:, 0:1024], AF.Abs)
            s.activation(tmp1[:, 0:1024], tmp1[:, 0:1024], AF.Exp, scale=-1.0)
            return s.activation(tmp1[:, 0:1024], tmp1[:, 0:1024], AF.Ln, bias=1.0)
        tk = ch.step("scalar", f_s, after=tk)

        def f_v2(v):
            v.tensor_scalar_min(tmp2[:, 0:1024], gkp[:, 0:1024], 0.0)
            v.tensor_sub(tmp2[:, 0:1024], tmp2[:, 0:1024], tmp1[:, 0:1024])
            return v.tensor_scalar_mul(gkb[:, 0:1024], tmp2[:, 0:1024], 1.0 / GLA_GLN)
        tk = ch.step("vector", f_v2, after=tk)

        def f_t2(t):
            t.matmul(ps[0].ap()[:, 0:512], C.tril64, gkb[:, 0:512], start=True, stop=True)
            return t.matmul(ps[1].ap()[:, 0:512], C.tril64, gkb[:, 512:1024], start=True, stop=True)
        tk = ch.step("tensor", f_t2, after=tk)

        def f_t3(t, tt=tt):
            t.matmul(ps[2].ap()[0:1, 0:512], C.onescolb[0:64, 0:1], gkb[0:64, 0:512],
                     start=True, stop=True)
            t.matmul(ps[3].ap()[0:1, 0:512],
                     C.onescolb[0:64, 0:1], gkb[0:64, 512:1024], start=True, stop=True)
            t.matmul(ps[4].ap()[0:1, 0:512], C.onescolb[64:128, 0:1], gkb[64:128, 0:512],
                     start=True, stop=True)
            return t.matmul(ps[5].ap()[0:1, 0:512], C.onescolb[64:128, 0:1],
                            gkb[64:128, 512:1024], start=True, stop=True)
        tkb = ch.step("tensor", f_t3, after=tk)

        def f_vb(v):
            v.tensor_copy(rowt[0:1, 0:512], ps[2].ap()[0:1, 0:512])
            v.tensor_copy(rowt[0:1, 512:1024], ps[3].ap()[0:1, 0:512])
            v.tensor_copy(tmp1[0:1, 0:512], ps[4].ap()[0:1, 0:512])
            return v.tensor_copy(tmp1[0:1, 512:1024], ps[5].ap()[0:1, 0:512])
        tkb = ch.step("vector", f_vb, after=tkb)
        tkb = ch.step("gpsimd", lambda g, tt=tt: g.dma_start(
            out=blrows[2 * tt:2 * tt + 1, 0:1024], in_=rowt[0:1, 0:1024]), after=tkb, dma=True)
        tkb = ch.step("gpsimd", lambda g, tt=tt: g.dma_start(
            out=blrows[2 * tt + 1:2 * tt + 2, 0:1024], in_=tmp1[0:1, 0:1024]), after=tkb, dma=True)

        def f_s2(s):
            s.activation(ebc[:, 0:512], ps[0].ap()[:, 0:512], AF.Exp)
            s.activation(ebc[:, 512:1024], ps[1].ap()[:, 0:512], AF.Exp)
            s.activation(embc[:, 0:512], ps[0].ap()[:, 0:512], AF.Exp, scale=-1.0)
            return s.activation(embc[:, 512:1024], ps[1].ap()[:, 0:512], AF.Exp, scale=-1.0)
        tk = ch.step("scalar", f_s2, after=tkb)

        def f_v3(v, tt=tt):
            v.tensor_mul(atile[:, 0:1024], qtile[:, :], ebc[:, 0:1024])
            return v.tensor_mul(b0tile[:, :], ktile[:, :], embc[:, 0:1024])
        tk = ch.step("vector", f_v3, after=ch.join(tk, ldq, ldk))
        ch.step("sync", lambda y, tt=tt: y.dma_start(
            out=C.b0tok[tt * 128:(tt + 1) * 128, :], in_=b0tile[:, :]), after=tk, dma=True)

        def f_t4(t):
            t.matmul(ps[0].ap()[:, 0:512], C.suf16, atile[:, 0:512], start=True, stop=True)
            return t.matmul(ps[1].ap()[:, 0:512], C.suf16, atile[:, 512:1024],
                            start=True, stop=True)
        tk = ch.step("tensor", f_t4, after=tk)

        def f_v4(v, tt=tt):
            v.tensor_mul(mtile[:, 0:512], ps[0].ap()[:, 0:512], b0tile[:, 0:512])
            v.tensor_mul(mtile[:, 512:1024], ps[1].ap()[:, 0:512], b0tile[:, 512:1024])
            return v.reduce_sum(colsum[:, tt * 8:(tt + 1) * 8],
                                mtile[:, 0:1024].rearrange("p (h d) -> p h d", d=128),
                                mybir.AxisListType.X)
        tk = ch.step("vector", f_v4, after=tk)

    for h in range(8):
        tk1 = ch.step("tensor", lambda t, h=h: t.transpose(
            ps[6].ap()[:, 0:32], blrows[0:32, h * 128:(h + 1) * 128], C.identf[0:32, 0:32]))
        ch.step("scalar", lambda s, h=h: s.activation(
            eblT[:, h * 32:(h + 1) * 32], ps[6].ap()[:, 0:32], AF.Exp), after=tk1)

    if C.dout(f"gla_blr{bi}", (128, 1024)) is not None:
        ch.step("sync", lambda y: y.dma_start(out=C.dbg[f"gla_blr{bi}"], in_=blrows[:, :]), dma=True)
    if C.dout(f"gla_cs{bi}", (T, 8)) is not None:
        dst = C.dbg[f"gla_cs{bi}"]
        ch.step("sync", lambda y, dst=dst: y.dma_start(
            out=dst.rearrange("(tt p) n -> p tt n", p=128),
            in_=colsum[:, :].rearrange("p (tt n) -> p tt n", n=8)), dma=True)
    if C.dout(f"gla_ebl{bi}", (128, 256)) is not None:
        ch.step("sync", lambda y: y.dma_start(out=C.dbg[f"gla_ebl{bi}"], in_=eblT[:, :]), dma=True)
    if STOP == "gla_pre":
        P.release(m2)
        P.release(m0)
        return

    vt2 = P.alloc_bf16(2048, "gla_v2")
    qT = P.alloc_bf16(8 * 64, "gla_qT")
    Otile = P.alloc_f32(2048, "gla_O")
    gtile = P.alloc_f32(2048, "gla_gt")
    lnmu = P.alloc_f32(8, "gla_lmu")
    lnrs = P.alloc_f32(8, "gla_lrs")
    sqb = P.alloc_f32(2048, "gla_sqb")
    silub = P.alloc_bf16(2048, "gla_sil")
    siluT = P.alloc_bf16(16 * 512, "gla_silT")
    wocd = P.alloc_bf16(16 * 128, "gla_wocd")
    x1t = P.alloc_f32(512, "gla_x1")

    if SCAN_TILES > -1 and 'nomemset' not in SCAN_PARTS:
        ch.step("vector", lambda v: v.memset(S[:, :], 0.0), after=ch.zero())

    for tt in range(SCAN_TILES):
        ldv = ch.step("sync", lambda y, tt=tt: y.dma_start(
            out=vt2[:, 0:2048], in_=C.vtb[tt * 128:(tt + 1) * 128, 0:2048]), dma=True)
        ldq = ch.step("sync", lambda y, tt=tt: y.dma_start(
            out=qtile[:, :], in_=C.qatok[tt * 128:(tt + 1) * 128, :]), dma=True)
        ldb = ch.step("sync", lambda y, tt=tt: y.dma_start(
            out=b0tile[:, :], in_=C.b0tok[tt * 128:(tt + 1) * 128, :]), dma=True)
        for half in range(2 if 'chunks' in SCAN_PARTS else 0):
            cidx = 2 * tt + half
            r0 = half * 64
            for h in range(8 if 'qt' in SCAN_SUB else 0):
                tk1 = ch.step("tensor", lambda t, h=h, r0=r0: t.transpose(
                    ps[6].ap().bitcast(BF16)[:, 0:64],
                    qtile[r0:r0 + 64, h * 128:(h + 1) * 128],
                    C.identb[r0:r0 + 64, r0:r0 + 64]), after=None)
                ch.step("vector", lambda v, h=h, cidx=cidx: v.tensor_scalar_mul(
                    qT[:, h * 64:(h + 1) * 64], ps[6].ap().bitcast(BF16)[:, 0:64],
                    eblT[:, h * 32 + cidx: h * 32 + cidx + 1]), after=tk1)
            for h in range(8):
                if 'kv' in SCAN_SUB:
                    tkv = ch.step("tensor", lambda t, h=h, r0=r0: t.matmul(
                        ps[7].ap()[:, 0:256],
                        b0tile[r0:r0 + 64, h * 128:(h + 1) * 128],
                        vt2[r0:r0 + 64, h * 256:(h + 1) * 256], start=True, stop=True))
                    tks = ch.step("vector", lambda v, h=h: v.tensor_add(
                        S[:, h * 256:(h + 1) * 256], S[:, h * 256:(h + 1) * 256],
                        ps[7].ap()[:, 0:256]), after=tkv)
                if 'smul' in SCAN_SUB:
                    tks = ch.step("scalar", lambda s, h=h, cidx=cidx: s.mul(
                        S[:, h * 256:(h + 1) * 256], S[:, h * 256:(h + 1) * 256],
                        eblT[:, h * 32 + cidx: h * 32 + cidx + 1]), after=None)
                if 'oi' in SCAN_SUB:
                    tko = ch.step("tensor", lambda t, h=h: t.matmul(
                        ps[5].ap()[0:64, 0:256], qT[:, h * 64:(h + 1) * 64],
                        S[:, h * 256:(h + 1) * 256].bitcast(BF16).rearrange(
                            "p (e two) -> p e two", two=2)[:, :, 1:2].squeeze(2),
                        start=True, stop=True), after=None)
                    ch.step("vector", lambda v, h=h, r0=r0, tt=tt: v.scalar_tensor_tensor(
                        Otile[r0:r0 + 64, h * 256:(h + 1) * 256],
                        vt2[r0:r0 + 64, h * 256:(h + 1) * 256],
                        colsum[r0:r0 + 64, tt * 8 + h: tt * 8 + h + 1],
                        ps[5].ap()[0:64, 0:256], ALU.mult, ALU.add), after=tko)

        if tt == 0 and C.dout(f"gla_b0t0_{bi}", (128, 1024), BF16) is not None:
            ch.step("sync", lambda y: y.dma_start(
                out=C.dbg[f"gla_b0t0_{bi}"], in_=b0tile[:, :]), dma=True)
        if tt == 0 and C.dout(f"gla_vt0_{bi}", (128, 2048), BF16) is not None:
            ch.step("sync", lambda y: y.dma_start(
                out=C.dbg[f"gla_vt0_{bi}"], in_=vt2[:, :]), dma=True)
        if tt == 0 and C.dout(f"gla_eblT_{bi}", (128, 256)) is not None:
            ch.step("sync", lambda y: y.dma_start(
                out=C.dbg[f"gla_eblT_{bi}"], in_=eblT[:, :]), dma=True)
        if tt == 0 and C.dout(f"gla_o_t0_{bi}", (128, 2048)) is not None:
            ch.step("sync", lambda y: y.dma_start(
                out=C.dbg[f"gla_o_t0_{bi}"], in_=Otile[:, :]), dma=True)
        if tt == 0 and C.dout(f"gla_S0_{bi}", (128, 2048)) is not None:
            ch.step("sync", lambda y: y.dma_start(
                out=C.dbg[f"gla_S0_{bi}"], in_=S[:, :]), dma=True)
        if 'epi' not in SCAN_PARTS:
            continue
        ldg = ch.step("sync", lambda y, tt=tt: y.dma_start(
            out=gtile[:, 0:2048], in_=C.gbuf[tt * 128:(tt + 1) * 128, 0:2048]), dma=True)
        tk = ch.step("vector", lambda v: v.tensor_add(Otile[:, :], Otile[:, :],
                                                      gtile[:, 0:2048]))
        tk2 = ch.step("scalar", lambda s: s.activation(sqb[:, 0:2048], Otile[:, :],
                                                       AF.Square), after=tk)

        def f_ln(v):
            v.reduce_sum(lnmu[:, 0:8], Otile[:, :].rearrange("p (h e) -> p h e", e=256),
                         mybir.AxisListType.X)
            v.reduce_sum(lnrs[:, 0:8], sqb[:, 0:2048].rearrange("p (h e) -> p h e", e=256),
                         mybir.AxisListType.X)
            v.tensor_scalar_mul(lnmu[:, 0:8], lnmu[:, 0:8], 1.0 / GLA_DV)
            v.tensor_scalar_mul(lnrs[:, 0:8], lnrs[:, 0:8], 1.0 / GLA_DV)
            v.tensor_mul(sqb[:, 0:8], lnmu[:, 0:8], lnmu[:, 0:8])
            v.tensor_sub(lnrs[:, 0:8], lnrs[:, 0:8], sqb[:, 0:8])
            return v.tensor_scalar_add(lnrs[:, 0:8], lnrs[:, 0:8], GLA_EPS)
        tk = ch.step("vector", f_ln, after=tk2)
        tk = ch.step("scalar", lambda s: s.activation(lnrs[:, 0:8], lnrs[:, 0:8], AF.Sqrt),
                     after=tk)
        tk = ch.step("vector", lambda v: v.reciprocal(lnrs[:, 0:8], lnrs[:, 0:8]), after=tk)

        def f_nrm(v):
            mu8 = lnmu[:, 0:8].unsqueeze(2).broadcast_to([128, 8, 256])
            rs8 = lnrs[:, 0:8].unsqueeze(2).broadcast_to([128, 8, 256])
            O3 = Otile[:, :].rearrange("p (h e) -> p h e", e=256)
            v.tensor_sub(O3, O3, mu8)
            v.tensor_mul(O3, O3, rs8)
            v.tensor_mul(Otile[:, :], Otile[:, :], C.alnwrep[:, 0:2048])
            return v.tensor_add(Otile[:, :], Otile[:, :], C.alnbrep[:, 0:2048])
        tk = ch.step("vector", f_nrm, after=tk)
        tk = ch.step("scalar", lambda s: s.activation(silub[:, 0:2048], Otile[:, :],
                                                      AF.Silu), after=tk)
        for cvd in range(16):
            tk1 = ch.step("tensor", lambda t, cvd=cvd: t.transpose(
                ps[6].ap().bitcast(BF16)[:, 0:128],
                silub[:, cvd * 128:(cvd + 1) * 128], C.identb),
                after=tk if cvd == 0 else None)
            ch.step("vector", lambda v, cvd=cvd, tt=tt: v.tensor_copy(
                siluT[:, cvd * 512 + (tt % 4) * 128: cvd * 512 + (tt % 4) * 128 + 128],
                ps[6].ap().bitcast(BF16)[:, 0:128]), after=tk1)

        if tt % 4 == 3 and 'wo' in SCAN_PARTS:
            grp = tt // 4
            for cd in range(16):
                ldwo = ch.step("gpsimd", lambda g, cd=cd: g.dma_start(
                    out=wocd[:, :].rearrange("p (c n) -> p c n", n=128),
                    in_=C.I["aWo"][:, cd * 128:(cd + 1) * 128].rearrange(
                        "(c p) n -> p c n", p=128)), dma=True)

                def f_wo(t, cd=cd):
                    r = None
                    for cvd in range(16):
                        r = t.matmul(ps[cd % 2].ap()[:, 0:512],
                                     wocd[:, cvd * 128:(cvd + 1) * 128],
                                     siluT[:, cvd * 512:(cvd + 1) * 512],
                                     start=(cvd == 0), stop=(cvd == 15))
                    return r
                tkw = ch.step("tensor", f_wo, after=ldwo)
                ldx = ch.step("sync", lambda y, cd=cd, grp=grp: y.dma_start(
                    out=x1t[:, 0:512],
                    in_=C.xT[bi].ap()[cd * 128:(cd + 1) * 128, grp * 512:(grp + 1) * 512]),
                    dma=True)
                tkv = ch.step("vector", lambda v, cd=cd: v.tensor_add(
                    x1t[:, 0:512], x1t[:, 0:512], ps[cd % 2].ap()[:, 0:512]),
                    after=ch.join(tkw, ldx))
                ch.step("gpsimd", lambda g, cd=cd, grp=grp: g.dma_start(
                    out=C.xT[bi].ap()[cd * 128:(cd + 1) * 128, grp * 512:(grp + 1) * 512],
                    in_=x1t[:, 0:512]), after=tkv, dma=True)

    if C.dout(f"gla_x1_{bi}", (D, T)) is not None:
        dst = C.dbg[f"gla_x1_{bi}"]
        ch.step("sync", lambda y, dst=dst: y.dma_start(out=dst, in_=C.xT[bi].ap()), dma=True)
    P.release(m2)
    P.release(m0)


def emit_ffn(C, bi, src_dram=None, tag=None, final=False):
    P = C.P
    ps = P.psum
    ch = P.chain
    if src_dram is None:
        src_dram = C.xT[bi]
    tag = tag or f"f{bi}"
    m0 = P.mark()

    fAcols = P.alloc_bf16(16 * 16, tag + "fA")
    fBt = P.alloc_bf16(FFN_F, tag + "fB")
    m16 = P.alloc_bf16(2048, tag + "m16")
    ch.step("gpsimd", lambda g: g.dma_start(
        out=fAcols[:, :].rearrange("p (c r) -> p c r", r=16),
        in_=C.I["fA"].rearrange("(c p) r -> p c r", p=128),
        allow_slow_non_contiguous=True), after=ch.zero(), dma=True)
    ch.step("sync", lambda y: y.dma_start(out=fBt[0:16, 0:FFN_F], in_=C.stg["fB"][:, :]),
            after=ch.zero(), waits=[(C.stg_done["fB"][0], C.stg_done["fB"][1])], dma=True)

    mxf = P.mark()
    xf = P.alloc_bf16(16 * 2048, tag + "xf")
    m1a = P.mark()
    u_t = P.alloc_f32(2 * 512, tag + "ub")
    h_t = P.alloc_bf16(2 * 512, tag + "hb")

    def mk_post(tb):
        def post(ch, c, tk):
            ch.step("sync", lambda y, c=c: y.dma_start(
                out=C.ubuf.ap()[c * 128:(c + 1) * 128, tb * 512:(tb + 1) * 512],
                in_=u_t[:, (c % 2) * 512:(c % 2 + 1) * 512]), after=tk, dma=True)
            return ch.step("gpsimd", lambda g, c=c: g.dma_start(
                out=C.hbuf.ap()[c * 128:(c + 1) * 128, tb * 512:(tb + 1) * 512],
                in_=h_t[:, (c % 2) * 512:(c % 2 + 1) * 512]), after=None, dma=True)
        return post
    for tb in range(4):
        emit_ln_band(C, ch, src_dram, tb, {
            "u": (lambda c: u_t[:, (c % 2) * 512:(c % 2 + 1) * 512]),
            "sc2": (C.cols["f_ln1"],
                    (lambda c, tb=tb: xf[:, c * 2048 + tb * 512: c * 2048 + (tb + 1) * 512]),
                    C.cols["f_ln2"],
                    (lambda c: h_t[:, (c % 2) * 512:(c % 2 + 1) * 512])),
            "post": mk_post(tb),
        }, name=f"{tag}ln{tb}")
    P.release(m1a)
    m1 = P.mark()

    wtile = P.alloc_bf16(16 * 128, tag + "wt")
    evb = P.alloc_bf16(512, tag + "ev")
    for Wname, dstd in (("fWq", C.qTb), ("fWk", C.kTb), ("fWv", C.vtb2)):
        for oc in range(16):
            ch.step("gpsimd", lambda g, Wname=Wname, oc=oc: g.dma_start(
                out=wtile[:, :].rearrange("p (c n) -> p c n", n=128),
                in_=C.stgT[Wname][oc, :, :, :].rearrange("c p n -> p c n")),
                waits=[(C.stg_done[Wname][0], C.stg_done[Wname][1])], dma=True)
            for t4 in range(4):
                bank = ps[t4 % 2]

                def f_t(t, t4=t4, bank=bank):
                    r = None
                    for c in range(DC):
                        r = t.matmul(bank.ap()[:, 0:512],
                                     wtile[:, c * 128:(c + 1) * 128],
                                     xf[:, c * 2048 + t4 * 512: c * 2048 + (t4 + 1) * 512],
                                     start=(c == 0), stop=(c == DC - 1))
                    return r
                tk = ch.step("tensor", f_t, after=None)
                if Wname == "fWq":
                    ev = ch.step("vector", lambda v, bank=bank: v.tensor_scalar_mul(
                        evb[:, :], bank.ap()[:, 0:512], 1.0 / float(FFN_HD) ** 0.5),
                        after=tk)
                else:
                    ev = ch.step("vector", lambda v, bank=bank: v.tensor_copy(
                        evb[:, :], bank.ap()[:, 0:512]), after=tk)
                ch.step("sync", lambda y, dstd=dstd, oc=oc, t4=t4: y.dma_start(
                    out=dstd.ap()[oc * 128:(oc + 1) * 128, t4 * 512:(t4 + 1) * 512],
                    in_=evb[:, :]), after=ev, dma=True)
    if C.dout(tag + "_qT", (D, T), BF16) is not None:
        ch.step("sync", lambda y: y.dma_start(out=C.dbg[tag + "_qT"], in_=C.qTb.ap()), dma=True)

    qhb = P.alloc_bf16(2048, tag + "qh")
    khb = P.alloc_bf16(2048, tag + "kh")
    vhb = P.alloc_bf16(2048, tag + "vh")
    vtk = P.alloc_bf16(16 * 128, tag + "vtk")
    sqs = P.alloc_f32(2048, tag + "sqs")
    prow = P.alloc_bf16(2048, tag + "prow")
    ptb = P.alloc_bf16(16 * 512, tag + "ptb")
    mhat = P.alloc_f32(16, tag + "mh")
    lcols = P.alloc_f32(8, tag + "lc")
    invl = P.alloc_f32(16, tag + "il")
    rowsc = P.alloc_f32(520, tag + "rsc")
    ivrep = P.alloc_f32(512, tag + "ivr")
    oevac = P.alloc_bf16(512, tag + "oev")
    ones128 = C.onescolf[0:1, 0:1].broadcast_to([1, 128])

    for h in range(FFN_H):
        ch.step("sync", lambda y, h=h: y.dma_start(
            out=qhb[:, :], in_=C.qTb.ap()[h * 128:(h + 1) * 128, :]), dma=True)
        ch.step("sync", lambda y, h=h: y.dma_start(
            out=khb[:, :], in_=C.kTb.ap()[h * 128:(h + 1) * 128, :]), dma=True)
        ch.step("sync", lambda y, h=h: y.dma_start(
            out=vhb[:, :], in_=C.vtb2.ap()[h * 128:(h + 1) * 128, :]), dma=True)
        for ki in range(16):
            tk1 = ch.step("tensor", lambda t, ki=ki: t.transpose(
                ps[6].ap().bitcast(BF16)[:, 0:128], vhb[:, ki * 128:(ki + 1) * 128],
                C.identb), after=None)
            ch.step("vector", lambda v, ki=ki: v.tensor_copy(
                vtk[:, ki * 128:(ki + 1) * 128], ps[6].ap().bitcast(BF16)[:, 0:128]),
                after=tk1)
        t1 = ch.step("scalar", lambda s: s.activation(sqs[:, :], qhb[:, :], AF.Square),
                     after=None)
        for b4 in range(4):
            tkm = ch.step("tensor", lambda t, b4=b4: t.matmul(
                ps[2].ap()[0:1, 0:512], C.onescolf[:, 0:1],
                sqs[:, b4 * 512:(b4 + 1) * 512], start=True, stop=True), after=None)
            ev = ch.step("vector", lambda v: v.tensor_copy(
                rowsc[0:1, 0:512], ps[2].ap()[0:1, 0:512]), after=tkm)
            for j in range(4):
                qt = b4 * 4 + j
                ch.step("tensor", lambda t, j=j, qt=qt: t.matmul(
                    ps[3].ap()[0:128, qt:qt + 1], rowsc[0:1, j * 128:(j + 1) * 128],
                    C.onescolf[0:1, 0:1], start=(qt == 0), stop=(qt == 15),
                    skip_group_check=True), after=ev)
        tk3 = ch.step("vector", lambda v: v.tensor_copy(mhat[:, 0:16],
                                                        ps[3].ap()[:, 0:16]), after=None)
        t2 = ch.step("scalar", lambda s: s.activation(sqs[:, :], khb[:, :], AF.Square),
                     after=tk3)
        for b4 in range(4):
            tkm = ch.step("tensor", lambda t, b4=b4: t.matmul(
                ps[2].ap()[0:1, 0:512], C.onescolf[:, 0:1],
                sqs[:, b4 * 512:(b4 + 1) * 512], start=True, stop=True), after=None)
            ch.step("vector", lambda v, b4=b4: v.tensor_copy(
                sqs[0:1, b4 * 512:(b4 + 1) * 512], ps[2].ap()[0:1, 0:512]), after=tkm)
        tk4 = ch.step("vector", lambda v: v.reduce_max(
            rowsc[0:1, 512:513], sqs[0:1, 0:2048], mybir.AxisListType.X), after=None)
        tk4 = ch.step("tensor", lambda t: t.matmul(
            ps[2].ap()[0:128, 0:1], rowsc[0:1, 512:513].broadcast_to([1, 128]),
            C.onescolf[0:1, 0:1], start=True, stop=True), after=tk4)
        tk4 = ch.step("vector", lambda v: v.tensor_copy(rowsc[:, 513:514],
                                                        ps[2].ap()[:, 0:1]), after=tk4)

        def f_mh(v):
            return v.tensor_mul(mhat[:, 0:16], mhat[:, 0:16],
                                rowsc[:, 513:514].broadcast_to([128, 16]))
        tk5 = ch.step("vector", f_mh, after=None)
        tk5 = ch.step("scalar", lambda s: s.activation(mhat[:, 0:16], mhat[:, 0:16],
                                                       AF.Sqrt), after=tk5)
        ch.step("vector", lambda v: v.tensor_scalar_mul(mhat[:, 0:16], mhat[:, 0:16],
                                                        -1.0), after=tk5)

        for g4 in range(4):
            for j in range(4):
                qt = g4 * 4 + j
                nkb = qt // 4 + 1
                for kb in range(nkb):
                    wid = 512 if kb < qt // 4 else (qt * 128 - kb * 512 + 128)
                    bank = ps[4 + kb % 2]
                    tkS = ch.step("tensor", lambda t, qt=qt, kb=kb, bank=bank, wid=wid:
                                  t.matmul(bank.ap()[:, 0:wid],
                                           qhb[:, qt * 128:(qt + 1) * 128],
                                           khb[:, kb * 512: kb * 512 + wid],
                                           start=True, stop=True), after=None)
                    if kb == qt // 4:
                        joff = qt * 128 - kb * 512
                        tkS = ch.step("vector", lambda v, bank=bank, joff=joff:
                                      v.tensor_add(bank.ap()[:, joff:joff + 128],
                                                   bank.ap()[:, joff:joff + 128],
                                                   C.cmask), after=tkS)
                    ch.step("scalar", lambda s, qt=qt, kb=kb, wid=wid, bank=bank:
                            s.activation(prow[:, kb * 512: kb * 512 + wid],
                                         bank.ap()[:, 0:wid], AF.Exp,
                                         bias=mhat[:, qt:qt + 1],
                                         accum_out=lcols[:, kb:kb + 1]), after=tkS)

                def f_l(v, qt=qt, nkb=nkb):
                    v.reduce_sum(invl[:, qt:qt + 1], lcols[:, 0:nkb],
                                 mybir.AxisListType.X)
                    return v.reciprocal(invl[:, qt:qt + 1], invl[:, qt:qt + 1])
                ch.step("vector", f_l, after=None)
                for ki in range(qt + 1):
                    tkT = ch.step("tensor", lambda t, ki=ki, j=j: t.matmul(
                        ps[6].ap().bitcast(BF16)[:, j * 128:(j + 1) * 128],
                        prow[:, ki * 128:(ki + 1) * 128], C.identb,
                        is_transpose=True, start=True, stop=True,
                        skip_group_check=True), after=None)
                    ch.step("vector", lambda v, ki=ki, j=j: v.tensor_copy(
                        ptb[:, ki * 512 + j * 128: ki * 512 + (j + 1) * 128],
                        ps[6].ap().bitcast(BF16)[:, j * 128:(j + 1) * 128]), after=tkT)
            for ki in range(g4 * 4 + 4):
                qlo = max(ki, g4 * 4)
                col0 = (qlo - g4 * 4) * 128
                ch.step("tensor", lambda t, ki=ki, col0=col0, g4=g4: t.matmul(
                    ps[7].ap()[:, col0:512], vtk[:, ki * 128:(ki + 1) * 128],
                    ptb[:, ki * 512 + col0: (ki + 1) * 512],
                    start=(ki == 0), stop=(ki == g4 * 4 + 3),
                    skip_group_check=True), after=None)
            for j in range(4):
                qt = g4 * 4 + j
                ch.step("tensor", lambda t, qt=qt, j=j: t.matmul(
                    ps[2].ap()[0:1, j * 128:(j + 1) * 128], invl[:, qt:qt + 1],
                    C.identf, start=(j == 0), stop=(j == 3), skip_group_check=True),
                    after=None)
            tkr = ch.step("vector", lambda v: v.tensor_copy(rowsc[0:1, 0:512],
                                                            ps[2].ap()[0:1, 0:512]),
                          after=None)
            tkr = ch.step("tensor", lambda t: t.matmul(
                ps[3].ap()[:, 0:512], ones128, rowsc[0:1, 0:512],
                start=True, stop=True), after=tkr)
            tkr = ch.step("vector", lambda v: v.tensor_copy(ivrep[:, 0:512],
                                                            ps[3].ap()[:, 0:512]),
                          after=tkr)
            tko = ch.step("vector", lambda v: v.tensor_mul(
                oevac[:, 0:512], ps[7].ap()[:, 0:512], ivrep[:, 0:512]), after=None)
            ch.step("sync", lambda y, h=h, g4=g4: y.dma_start(
                out=C.qTb.ap()[h * 128:(h + 1) * 128, g4 * 512:(g4 + 1) * 512],
                in_=oevac[:, 0:512]), after=tko, dma=True)

    P.release(mxf)
    if C.dout(tag + "_attn", (D, T), BF16) is not None:
        ch.step("sync", lambda y: y.dma_start(out=C.dbg[tag + "_attn"], in_=C.qTb.ap()), dma=True)
    if STOP == tag + "_att":
        P.release(m0)
        return

    m2 = P.mark()
    attno = P.alloc_bf16(16 * 512, tag + "ao")
    wtile2 = P.alloc_bf16(16 * 128, tag + "wt2")
    evb2 = P.alloc_bf16(512, tag + "ev2")
    for t4 in range(4):
        ch.step("sync", lambda y, t4=t4: y.dma_start(
            out=attno[:, :].rearrange("p (c t) -> p c t", t=512),
            in_=C.qTb.ap().rearrange("(c p) t -> p c t", p=128)[:, :, t4 * 512:(t4 + 1) * 512]),
            dma=True)
        for oc in range(16):
            ch.step("gpsimd", lambda g, oc=oc: g.dma_start(
                out=wtile2[:, :].rearrange("p (c n) -> p c n", n=128),
                in_=C.stgT["fWo"][oc, :, :, :].rearrange("c p n -> p c n")),
                waits=[(C.stg_done["fWo"][0], C.stg_done["fWo"][1])], dma=True)

            def f_t2(t, oc=oc):
                r = None
                for c in range(DC):
                    r = t.matmul(ps[oc % 2].ap()[:, 0:512],
                                 wtile2[:, c * 128:(c + 1) * 128],
                                 attno[:, c * 512:(c + 1) * 512],
                                 start=(c == 0), stop=(c == DC - 1))
                return r
            tk = ch.step("tensor", f_t2, after=None)
            ev = ch.step("vector", lambda v, oc=oc: v.tensor_copy(
                evb2[:, :], ps[oc % 2].ap()[:, 0:512]), after=tk)
            ch.step("sync", lambda y, oc=oc, t4=t4: y.dma_start(
                out=C.kTb.ap()[oc * 128:(oc + 1) * 128, t4 * 512:(t4 + 1) * 512],
                in_=evb2[:, :]), after=ev, dma=True)
    P.release(m2)
    if C.dout(tag + "_awo", (D, T), BF16) is not None:
        ch.step("sync", lambda y: y.dma_start(out=C.dbg[tag + "_awo"], in_=C.kTb.ap()), dma=True)
    if STOP == tag + "_wo":
        P.release(m0)
        return

    emit_ffn_mlp(C, src_dram, tag, fAcols, fBt, m16, final)
    if C.dout(tag + "_x2", (D, T)) is not None:
        ch.step("sync", lambda y: y.dma_start(out=C.dbg[tag + "_x2"], in_=src_dram.ap()), dma=True)
    P.release(m0)


def emit_ffn_mlp(C, src_dram, tag, fAcols, fBt, m16, final):
    P = C.P
    ps = P.psum
    ch = P.chain
    m0 = P.mark()
    hb2 = P.alloc_bf16(16 * 512, tag + "mhb")
    gu = P.alloc_bf16(64 * 512, tag + "gu")
    attno = gu[:, 0:16 * 512]
    wtile = P.alloc_bf16(16 * 128, tag + "mwt")
    wdt = P.alloc_bf16(32 * 128, tag + "mwd")
    silut = P.alloc_f32(512, tag + "msl")
    mrow = P.alloc_f32(512, tag + "mmr")
    ut = P.alloc_f32(512, tag + "mut")
    x1t = P.alloc_f32(512, tag + "mx1")
    x2t = P.alloc_f32(512, tag + "mx2")
    for t4 in range(4):
        ch.step("sync", lambda y, t4=t4: y.dma_start(
            out=attno[:, :].rearrange("p (c t) -> p c t", t=512),
            in_=C.kTb.ap().rearrange("(c p) t -> p c t", p=128)[:, :, t4 * 512:(t4 + 1) * 512]),
            dma=True)
        ch.step("sync", lambda y, t4=t4: y.dma_start(
            out=hb2[:, :].rearrange("p (c t) -> p c t", t=512),
            in_=C.hbuf.ap().rearrange("(c p) t -> p c t", p=128)[:, :, t4 * 512:(t4 + 1) * 512]),
            dma=True)

        def f_mA(t):
            r = None
            for c in range(DC):
                r = t.matmul(ps[2].ap()[0:16, 0:512], fAcols[:, c * 16:(c + 1) * 16],
                             attno[:, c * 512:(c + 1) * 512],
                             start=(c == 0), stop=(c == DC - 1))
            for c in range(DC):
                r = t.matmul(ps[3].ap()[0:16, 0:512], fAcols[:, c * 16:(c + 1) * 16],
                             hb2[:, c * 512:(c + 1) * 512],
                             start=(c == 0), stop=(c == DC - 1))
            return r
        tk = ch.step("tensor", f_mA, after=None)
        tk = ch.step("scalar", lambda s: s.activation(mrow[0:16, 0:512],
                                                      ps[2].ap()[0:16, 0:512],
                                                      AF.Tanh), after=tk)

        def f_m16(v, t4=t4):
            v.tensor_scalar_mul(mrow[0:16, 0:512], mrow[0:16, 0:512], FFN_MOD)
            return v.tensor_mul(m16[0:16, t4 * 512:(t4 + 1) * 512], mrow[0:16, 0:512],
                                ps[3].ap()[0:16, 0:512])
        tk = ch.step("vector", f_m16, after=tk)

        for fc in range(FFN_F // 128):
            ch.step("gpsimd", lambda g, fc=fc: g.dma_start(
                out=wtile[:, :].rearrange("p (c n) -> p c n", n=128),
                in_=C.stgT["fWin"][fc, :, :, :].rearrange("c p n -> p c n")),
                waits=[(C.stg_done["fWin"][0], C.stg_done["fWin"][1])], dma=True)

            def f_gate(t, fc=fc, t4=t4):
                for c in range(DC):
                    t.matmul(ps[4].ap()[:, 0:512], wtile[:, c * 128:(c + 1) * 128],
                             hb2[:, c * 512:(c + 1) * 512],
                             start=(c == 0), stop=False)
                return t.matmul(ps[4].ap()[:, 0:512], fBt[0:16, fc * 128:(fc + 1) * 128],
                                m16[0:16, t4 * 512:(t4 + 1) * 512],
                                start=False, stop=True)
            tkg = ch.step("tensor", f_gate, after=None)
            tkg = ch.step("scalar", lambda s: s.activation(silut[:, 0:512],
                                                           ps[4].ap()[:, 0:512],
                                                           AF.Silu), after=tkg)
            ch.step("gpsimd", lambda g, fc=fc: g.dma_start(
                out=wtile[:, :].rearrange("p (c n) -> p c n", n=128),
                in_=C.stgT["fWup"][fc, :, :, :].rearrange("c p n -> p c n")),
                waits=[(C.stg_done["fWup"][0], C.stg_done["fWup"][1])], dma=True)

            def f_up(t, fc=fc):
                r = None
                for c in range(DC):
                    r = t.matmul(ps[5].ap()[:, 0:512], wtile[:, c * 128:(c + 1) * 128],
                                 hb2[:, c * 512:(c + 1) * 512],
                                 start=(c == 0), stop=(c == DC - 1))
                return r
            tku = ch.step("tensor", f_up, after=None)
            ch.step("vector", lambda v, fc=fc: v.tensor_mul(
                gu[:, fc * 512:(fc + 1) * 512], silut[:, 0:512], ps[5].ap()[:, 0:512]),
                after=ch.join(tkg, tku))

        for dch in range(16):
            tkd = None
            for fh in range(2):
                ch.step("gpsimd", lambda g, dch=dch, fh=fh: g.dma_start(
                    out=wdt[:, :].rearrange("p (c n) -> p c n", n=128),
                    in_=C.stgT["fWdown"][dch, fh * 32:(fh + 1) * 32, :, :].rearrange(
                        "c p n -> p c n")),
                    waits=[(C.stg_done["fWdown"][0], C.stg_done["fWdown"][1])], dma=True)

                def f_dn(t, dch=dch, fh=fh):
                    r = None
                    for fc in range(32):
                        r = t.matmul(ps[6 + dch % 2].ap()[:, 0:512],
                                     wdt[:, fc * 128:(fc + 1) * 128],
                                     gu[:, (fh * 32 + fc) * 512:(fh * 32 + fc + 1) * 512],
                                     start=(fh == 0 and fc == 0),
                                     stop=(fh == 1 and fc == 31))
                    return r
                tkd = ch.step("tensor", f_dn, after=None)
            lu = ch.step("sync", lambda y, dch=dch, t4=t4: y.dma_start(
                out=ut[:, 0:512],
                in_=C.ubuf.ap()[dch * 128:(dch + 1) * 128, t4 * 512:(t4 + 1) * 512]),
                dma=True)
            lx = ch.step("sync", lambda y, dch=dch, t4=t4: y.dma_start(
                out=x1t[:, 0:512],
                in_=src_dram.ap()[dch * 128:(dch + 1) * 128, t4 * 512:(t4 + 1) * 512]),
                dma=True)

            def f_res(v, dch=dch):
                v.tensor_add(x2t[:, 0:512], ps[6 + dch % 2].ap()[:, 0:512], ut[:, 0:512])
                return v.tensor_add(x2t[:, 0:512], x2t[:, 0:512], x1t[:, 0:512])
            tkv = ch.step("vector", f_res, after=ch.join(tkd, lu, lx))
            if not final:
                ch.step("gpsimd", lambda g, dch=dch, t4=t4: g.dma_start(
                    out=src_dram.ap()[dch * 128:(dch + 1) * 128, t4 * 512:(t4 + 1) * 512],
                    in_=x2t[:, 0:512]), after=tkv, dma=True)
            else:
                for j in range(4):
                    tkt = ch.step("tensor", lambda t, j=j: t.transpose(
                        ps[4].ap()[0:128, 0:128],
                        x2t[:, j * 128:(j + 1) * 128], C.identf),
                        after=tkv if j == 0 else None)
                    ev = ch.step("vector", lambda v: v.tensor_copy(
                        ut[:, 0:128], ps[4].ap()[0:128, 0:128]), after=tkt)
                    ch.step("sync", lambda y, dch=dch, t4=t4, j=j: y.dma_start(
                        out=C.out[t4 * 512 + j * 128 - C.out_row0:
                                  t4 * 512 + (j + 1) * 128 - C.out_row0,
                                  dch * 128:(dch + 1) * 128],
                        in_=ut[:, 0:128]), after=ev, dma=True)
    P.release(m0)


def emit_poll_and_flag(C, ch, phase, name):
    P = C.P
    psem = C.pollsem
    uid = P.lbl(name)

    def f(g, semh, phase=phase, uid=uid):
        eng = OrderedEngineSet([g.engine])
        with g.register(uid + "_nv") as r_t:
            g.reg_load(r_t, C.metat[0:1, 1:2])
            g.reg_alu(r_t, r_t, phase, ALU.add)
            g.reg_save(C.flagtile[0:1, 0:1], r_t)

        def side_body(side):
            g.dma_start(out=C.shflag.ap()[side:side + 1, 0:1],
                        in_=C.flagtile[0:1, 0:1]).then_inc(semh, 16)
            lbl, done = f"{uid}_l{side}", f"{uid}_d{side}"
            g.bass.br(lbl, engines=eng)
            with g.bass.body(lbl, valid_engines=eng):
                g.reg_alu(C.rpw, C.rpw, 16, ALU.add)
                g.dma_start(out=C.flagtile[0:1, 4:5],
                            in_=C.shflag.ap()[1 - side:2 - side, 0:1]
                            ).then_inc(psem.h, 16)
                g.wait_ge(psem.h, C.rpw)
                with g.register(f"{uid}_f{side}") as r_fv, \
                     g.register(f"{uid}_t{side}") as r_tg:
                    g.reg_load(r_fv, C.flagtile[0:1, 4:5])
                    g.reg_load(r_tg, C.metat[0:1, 1:2])
                    g.reg_alu(r_tg, r_tg, phase, ALU.add)
                    if phase == 1:
                        g.bass.br_ne(r_fv, r_tg, on_true=lbl, on_false=done,
                                     engines=eng)
                    else:
                        g.bass.br_lt(r_fv, r_tg, on_true=lbl, on_false=done,
                                     engines=eng)
            g.bass.switch_bb(done)
        jbranch(g, C.rj, uid + "_b", side_body)
    ch.step("gpsimd", f, after=None, dma=True, inc=16, manual=True)
    return ch.step("gpsimd", lambda g: None, after=None)


def emit_exchange(C):
    P = C.P
    ch = P.chain
    ps = P.psum
    x2flat = C.x2all.ap().rearrange("b d t -> (b d) t")

    def f_pub(g, semh):
        def side_body(side):
            for bi in range(NB):
                b = 2 * side + bi
                g.dma_start(out=x2flat[b * 2048:(b + 1) * 2048, :],
                            in_=C.xT[bi].ap()).then_inc(semh, 16)
        jbranch(g, C.rj, P.lbl("xpub"), side_body)
    ch.step("gpsimd", f_pub, after=None, dma=True, inc=32, manual=True)
    emit_poll_and_flag(C, ch, 1, "exch")

    m0 = P.mark()
    u_t = P.alloc_f32(2 * 512, "exu")
    ub = P.alloc_bf16(2 * 512, "exub")
    for sl in range(B):
        loader = None
        for tb in range(4):
            def mk_post(sl, tb):
                def post(ch, c, tk):
                    tk = ch.step("sync", lambda y, c=c: y.dma_start(
                        out=C.x3loc.ap()[sl, c * 128:(c + 1) * 128,
                                         tb * 512:(tb + 1) * 512],
                        in_=u_t[:, (c % 2) * 512:(c % 2 + 1) * 512]), after=tk, dma=True)
                    tkc = ch.step("vector", lambda v, c=c: v.tensor_copy(
                        ub[:, (c % 2) * 512:(c % 2 + 1) * 512],
                        u_t[:, (c % 2) * 512:(c % 2 + 1) * 512]), after=tk)
                    last = tkc
                    for j in range(4):
                        tkt = ch.step("tensor", lambda t, c=c, j=j: t.transpose(
                            ps[6].ap().bitcast(BF16)[:, 0:128],
                            ub[:, (c % 2) * 512 + j * 128:(c % 2) * 512 + (j + 1) * 128],
                            C.identb), after=None)
                        tke = ch.step("vector", lambda v: v.tensor_copy(
                            ub[0:128, 0:64].bitcast(F32)[:, 0:64].bitcast(BF16)[:, 0:128]
                            if False else C.x3tr[:, 0:128],
                            ps[6].ap().bitcast(BF16)[:, 0:128]), after=tkt)
                        last = ch.step("sync", lambda y, sl=sl, c=c, j=j, tb=tb: y.dma_start(
                            out=C.x3tok.ap()[sl, tb * 512 + j * 128: tb * 512 + (j + 1) * 128,
                                             c * 128:(c + 1) * 128],
                            in_=C.x3tr[:, 0:128]), after=tke, dma=True)
                    return last
                return post
            emit_ln_band(C, ch, C.x2all.ap()[sl], tb,
                         {"u": (lambda c: u_t[:, (c % 2) * 512:(c % 2 + 1) * 512]),
                          "post": mk_post(sl, tb)},
                         name=f"x3l{sl}_{tb}")
    P.release(m0)
    if C.dout("x3_s0", (D, T)) is not None:
        ch.step("sync", lambda y: y.dma_start(out=C.dbg["x3_s0"],
                                              in_=C.x3loc.ap()[0]), dma=True)
    if C.dout("x3_s2", (D, T)) is not None:
        ch.step("sync", lambda y: y.dma_start(out=C.dbg["x3_s2"],
                                              in_=C.x3loc.ap()[2]), dma=True)
    if C.dout("x3t_s0", (T, D), BF16) is not None:
        ch.step("sync", lambda y: y.dma_start(out=C.dbg["x3t_s0"],
                                              in_=C.x3tok.ap()[0]), dma=True)


def emit_attn_b(C):
    P = C.P
    ch = P.chain
    ps = P.psum
    P.release(C.gla_mark)
    m0 = P.mark()
    NMC = 8
    invN = 2.0 / float(TB_ROWS * D)

    W0T = P.alloc_f32(16 * 1024, "bW0T")
    W1T = P.alloc_f32(8 * 2048, "bW1T")
    w1nb = P.alloc_f32(1024, "bw1nb", align=8).bitcast(BF16)
    b0c = P.alloc_f32(8, "bb0")
    b1c = P.alloc_f32(16, "bb1")
    db1c = P.alloc_f32(16, "bdb1")
    scal = P.alloc_f32(16, "bscal")
    cme = P.alloc_f32(64, "bcme")

    bfv = lambda ap: ap.bitcast(BF16).rearrange(
        "p (n two) -> p n two", two=2)[:, :, 1:2].squeeze(2)

    mi = P.mark()
    ldbuf = P.alloc_f32(2048, "bldb")
    for mc in range(NMC):
        lt = ch.step("sync", lambda y, mc=mc: y.dma_start(
            out=ldbuf[:, 0:2048], in_=C.I["bW0h"][mc * 128:(mc + 1) * 128, :]), dma=True)
        for dc in range(16):
            t1 = ch.step("tensor", lambda t, dc=dc: t.transpose(
                ps[6].ap()[0:128, 0:128], ldbuf[:, dc * 128:(dc + 1) * 128],
                C.identf), after=None)
            ch.step("vector", lambda v, dc=dc, mc=mc: v.tensor_copy(
                W0T[:, dc * 1024 + mc * 128: dc * 1024 + (mc + 1) * 128],
                ps[6].ap()[0:128, 0:128]), after=t1)
    for dc in range(16):
        lt = ch.step("sync", lambda y, dc=dc: y.dma_start(
            out=ldbuf[:, 0:1024], in_=C.I["bW1h"][dc * 128:(dc + 1) * 128, :]), dma=True)
        for mc in range(NMC):
            t1 = ch.step("tensor", lambda t, mc=mc: t.transpose(
                ps[6].ap()[0:128, 0:128], ldbuf[:, mc * 128:(mc + 1) * 128],
                C.identf), after=None)
            ch.step("vector", lambda v, dc=dc, mc=mc: v.tensor_copy(
                W1T[:, mc * 2048 + dc * 128: mc * 2048 + (dc + 1) * 128],
                ps[6].ap()[0:128, 0:128]), after=t1)
    ch.step("vector", lambda v: v.tensor_copy(b0c[:, 0:8], C.cols["b_b0h"][:, 0:8]),
            after=None)
    ch.step("vector", lambda v: v.tensor_copy(b1c[:, 0:16], C.cols["b_b1"]), after=None)
    P.release(mi)

    m_step = P.mark()
    chTbreg = P.alloc_f32(4096, "bchreg")
    chTb = chTbreg.bitcast(BF16)
    predb = chTb
    dpb = chTb
    w1stg = chTbreg[:, 0:1024].bitcast(BF16)
    prtlreg = P.alloc_f32(4096, "bprtl")
    prtl = prtlreg.bitcast(BF16)
    dhpb = prtlreg[:, 0:2048].bitcast(BF16)
    dptokq = prtlreg[:, 2048:3072].bitcast(BF16)
    uscr = prtlreg[:, 3072:3584]
    ptile = prtlreg[:, 3584:4096]
    rowx = uscr
    hb = P.alloc_bf16(8 * 512, "bhb")
    dsbreg = P.alloc_f32(2048, "bdsb")
    dsb = dsbreg.bitcast(BF16)
    dhptok = dsbreg.bitcast(BF16)
    htok = P.alloc_bf16(4 * 1024, "bhtok")
    ctile = P.alloc_f32(512, "bct")
    w512 = P.alloc_f32(512, "bw512")
    chp = P.alloc_bf16(16 * 128, "bchp")
    chkr = P.alloc_bf16(2 * 128, "bchkr")

    def upd_w1tb():
        for mc in range(NMC):
            tkc = ch.step("vector", lambda v, mc=mc: v.tensor_copy(
                w1stg[:, 0:2048], W1T[:, mc * 2048:(mc + 1) * 2048]), after=None)
            ch.step("sync", lambda y, mc=mc: y.dma_start(
                out=C.w1tb.ap()[mc * 128:(mc + 1) * 128, :],
                in_=w1stg[:, 0:2048]), after=tkc, dma=True)
    upd_w1tb()

    shx = C.shexch.ap()

    for i in range(TB_N):
        par = i % 2
        for bj in range(4):
            ch.step("gpsimd", lambda g, i=i, bj=bj: g.dma_start(
                out=chTb[:, :].rearrange("p (c b t) -> p c b t", b=4, t=128)[:, :, bj],
                in_=C.x3loc.ap()[bj][:, i * 128:(i + 1) * 128].rearrange(
                    "(c p) t -> p c t", p=128)), dma=True)
        ldtok = ch.tok()

        def f_lr(t):
            r = None
            for dc in range(16):
                r = t.matmul(ps[2].ap()[0:1, 0:512], C.colsb["b_lrw"][:, dc:dc + 1],
                             chTb[:, dc * 512:(dc + 1) * 512],
                             start=(dc == 0), stop=(dc == 15))
            return r
        tk = ch.step("tensor", f_lr, after=ldtok)
        tk = ch.step("scalar", lambda s: s.activation(
            rowx[0:1, 0:512], ps[2].ap()[0:1, 0:512], AF.Sigmoid,
            bias=C.cols["b_lrb"][0:1, 0:1]), after=tk)

        def f_lrm(v):
            v.reduce_sum(scal[0:1, 0:1], rowx[0:1, 0:512], mybir.AxisListType.X)
            return v.tensor_scalar_mul(scal[0:1, 0:1], scal[0:1, 0:1],
                                       TB_LR / TB_ROWS)
        tk = ch.step("vector", f_lrm, after=tk)
        tk = ch.step("tensor", lambda t: t.matmul(
            ps[2].ap()[0:128, 0:1], scal[0:1, 0:1].broadcast_to([1, 128]),
            C.onescolf[0:1, 0:1], start=True, stop=True), after=tk)
        tk = ch.step("vector", lambda v: v.tensor_scalar_mul(
            scal[:, 1:2], ps[2].ap()[:, 0:1], -1.0), after=tk)

        tk2 = ch.step("vector", lambda v: v.reduce_sum(
            cme[:, 0:64].rearrange("p (c b) -> p c b", b=4),
            chTb[:, :].rearrange("p (c b t) -> p (c b) t", b=4, t=128),
            mybir.AxisListType.X), after=ldtok)
        tk2 = ch.step("vector", lambda v: v.tensor_scalar_mul(
            cme[:, 0:64], cme[:, 0:64], 1.0 / TB_C), after=tk2)
        tk2 = ch.step("vector", lambda v: v.tensor_copy(
            rowx[:, 0:32].bitcast(BF16)[:, 0:64], cme[:, 0:64]), after=tk2)

        def f_fg(t):
            r = None
            for dc in range(16):
                r = t.matmul(ps[3].ap()[0:1, 0:4], C.colsb["b_fgw"][:, dc:dc + 1],
                             rowx[:, 0:32].bitcast(BF16)[:, dc * 4:(dc + 1) * 4],
                             start=(dc == 0), stop=(dc == 15))
            return r
        tk2 = ch.step("tensor", f_fg, after=tk2)
        tk2 = ch.step("scalar", lambda s: s.activation(
            rowx[0:1, 0:4], ps[3].ap()[0:1, 0:4], AF.Sigmoid,
            bias=C.cols["b_fgb"][0:1, 0:1]), after=tk2)

        def f_fgm(v):
            v.reduce_sum(scal[0:1, 2:3], rowx[0:1, 0:4], mybir.AxisListType.X)
            return v.tensor_scalar_mul(scal[0:1, 2:3], scal[0:1, 2:3], 0.25)
        tk2 = ch.step("vector", f_fgm, after=tk2)
        tk2 = ch.step("tensor", lambda t: t.matmul(
            ps[3].ap()[0:128, 0:1], scal[0:1, 2:3].broadcast_to([1, 128]),
            C.onescolf[0:1, 0:1], start=True, stop=True), after=tk2)
        tk2 = ch.step("vector", lambda v: v.tensor_copy(scal[:, 3:4],
                                                        ps[3].ap()[:, 0:1]), after=tk2)

        for mc in range(NMC):
            def f_h(t, mc=mc):
                r = None
                for dc in range(16):
                    r = t.matmul(ps[mc % 2].ap()[:, 0:512],
                                 bfv(W0T[:, dc * 1024 + mc * 128: dc * 1024 + (mc + 1) * 128]),
                                 chTb[:, dc * 512:(dc + 1) * 512],
                                 start=(dc == 0), stop=(dc == 15))
                return r
            tkh = ch.step("tensor", f_h, after=None)
            tkh = ch.step("vector", lambda v, mc=mc: v.tensor_scalar_add(
                uscr[:, 0:512], ps[mc % 2].ap()[:, 0:512],
                b0c[:, mc:mc + 1]), after=tkh)
            tkh = ch.step("scalar", lambda s, mc=mc: s.activation(
                hb[:, mc * 512:(mc + 1) * 512], uscr[:, 0:512], AF.Silu), after=tkh)
            tkh = ch.step("scalar", lambda s, mc=mc: s.activation(
                dsb[:, mc * 512:(mc + 1) * 512], uscr[:, 0:512],
                AF.Sigmoid), after=tkh)

            def f_ds(v, mc=mc):
                sl = slice(mc * 512, (mc + 1) * 512)
                v.tensor_mul(w512[:, 0:512], hb[:, sl], dsb[:, sl])
                v.tensor_sub(w512[:, 0:512], hb[:, sl], w512[:, 0:512])
                return v.tensor_add(dsb[:, sl], dsb[:, sl], w512[:, 0:512])
            ch.step("vector", f_ds, after=tkh)

        for dc in range(16):
            def f_p(t, dc=dc):
                r = None
                for mc in range(NMC):
                    r = t.matmul(ps[2 + dc % 2].ap()[:, 0:512],
                                 bfv(W1T[:, mc * 2048 + dc * 128: mc * 2048 + (dc + 1) * 128]),
                                 hb[:, mc * 512:(mc + 1) * 512],
                                 start=(mc == 0), stop=(mc == NMC - 1))
                return r
            tkp = ch.step("tensor", f_p, after=None)
            ch.step("vector", lambda v, dc=dc: v.tensor_copy(
                predb[:, dc * 512:(dc + 1) * 512], ps[2 + dc % 2].ap()[:, 0:512]),
                after=tkp)

        def f_pub(g, semh, par=par):
            def side_body(side):
                g.dma_start(
                    out=shx[side, par].rearrange("(c p) t -> p c t", p=128),
                    in_=predb[:, :].rearrange("p (c t) -> p c t", t=512)
                ).then_inc(semh, 16)
            jbranch(g, C.rj, P.lbl("bpub"), side_body)
        ch.step("gpsimd", f_pub, after=None, dma=True, inc=16, manual=True)
        emit_poll_and_flag(C, ch, 2 + i, f"bst{i}")

        def f_rd(g, semh, par=par):
            def side_body(side):
                g.dma_start(
                    out=prtl[:, :].rearrange("p (c t) -> p c t", t=512),
                    in_=shx[1 - side, par].rearrange("(c p) t -> p c t", p=128)
                ).then_inc(semh, 16)
            jbranch(g, C.rj, P.lbl("brd"), side_body)
        ch.step("gpsimd", f_rd, after=None, dma=True, inc=16, manual=True)

        for dc in range(16):
            lc = ch.step("sync", lambda y, dc=dc, i=i: y.dma_start(
                out=ctile[:, 0:512].rearrange("p (b t) -> p b t", t=128),
                in_=C.x3loc.ap()[:, dc * 128:(dc + 1) * 128, i * 128:(i + 1) * 128].rearrange(
                    "b p t -> p b t")), dma=True)

            def f_dp(v, dc=dc):
                v.tensor_add(w512[:, 0:512], prtl[:, dc * 512:(dc + 1) * 512],
                             predb[:, dc * 512:(dc + 1) * 512])
                v.scalar_tensor_tensor(w512[:, 0:512], w512[:, 0:512],
                                       b1c[:, dc:dc + 1], ctile[:, 0:512],
                                       ALU.add, ALU.subtract)
                v.reduce_sum(db1c[:, dc:dc + 1], w512[:, 0:512], mybir.AxisListType.X)
                return v.tensor_scalar_mul(dpb[:, dc * 512:(dc + 1) * 512],
                                           w512[:, 0:512], invN)
            ch.step("vector", f_dp, after=None)
        ch.step("vector", lambda v: v.tensor_scalar_mul(
            db1c[:, 0:16], db1c[:, 0:16], invN), after=None)

        for dc in range(16):
            lb = ch.step("scalar", lambda s2, dc=dc: s2.dma_start(
                out=w1nb[:, 0:1024],
                in_=C.w1tb.ap()[:, dc * 128:(dc + 1) * 128], transpose=True),
                dma=True)
            for mc in range(NMC):
                ch.step("tensor", lambda t, dc=dc, mc=mc: t.matmul(
                    ps[mc].ap()[:, 0:512],
                    w1nb[:, mc * 128:(mc + 1) * 128],
                    dpb[:, dc * 512:(dc + 1) * 512],
                    start=(dc == 0), stop=(dc == 15)), after=None)
        for mc in range(NMC):
            tkd = ch.step("vector", lambda v, mc=mc: v.tensor_mul(
                uscr[:, 0:512], ps[mc].ap()[:, 0:512],
                dsb[:, mc * 512:(mc + 1) * 512]), after=None)
            tkd = ch.step("vector", lambda v, mc=mc: v.reduce_sum(
                scal[:, 4 + mc:5 + mc], uscr[:, 0:512], mybir.AxisListType.X),
                after=tkd)
            ch.step("vector", lambda v, mc=mc: v.tensor_copy(
                dhpb[:, mc * 512:(mc + 1) * 512], uscr[:, 0:512]), after=tkd)

        for tj in range(4):
            for mc in range(NMC):
                t1 = ch.step("tensor", lambda t, tj=tj, mc=mc: t.transpose(
                    ps[6].ap().bitcast(BF16)[:, 0:128],
                    hb[:, mc * 512 + tj * 128: mc * 512 + (tj + 1) * 128],
                    C.identb), after=None)
                ch.step("vector", lambda v, tj=tj, mc=mc: v.tensor_copy(
                    htok[:, tj * 1024 + mc * 128: tj * 1024 + (mc + 1) * 128],
                    ps[6].ap().bitcast(BF16)[:, 0:128]), after=t1)
                t2 = ch.step("tensor", lambda t, tj=tj, mc=mc: t.transpose(
                    ps[6].ap().bitcast(BF16)[:, 0:128],
                    dhpb[:, mc * 512 + tj * 128: mc * 512 + (tj + 1) * 128],
                    C.identb), after=None)
                ch.step("vector", lambda v, tj=tj, mc=mc: v.tensor_copy(
                    dhptok[:, tj * 1024 + mc * 128: tj * 1024 + (mc + 1) * 128],
                    ps[6].ap().bitcast(BF16)[:, 0:128]), after=t2)

        for db4 in range(4):
            for tj in range(4):
                for dcq in range(4):
                    dc = db4 * 4 + dcq
                    t1 = ch.step("tensor", lambda t, tj=tj, dc=dc: t.transpose(
                        ps[6].ap().bitcast(BF16)[:, 0:128],
                        dpb[:, dc * 512 + tj * 128: dc * 512 + (tj + 1) * 128],
                        C.identb), after=None)
                    ch.step("vector", lambda v, tj=tj, dcq=dcq: v.tensor_copy(
                        dptokq[:, tj * 512 + dcq * 128: tj * 512 + (dcq + 1) * 128],
                        ps[6].ap().bitcast(BF16)[:, 0:128]), after=t1)
            for mc in range(NMC):
                def f_w1(t, mc=mc):
                    r = None
                    for tj in range(4):
                        r = t.matmul(ps[4 + mc % 2].ap()[:, 0:512],
                                     htok[:, tj * 1024 + mc * 128: tj * 1024 + (mc + 1) * 128],
                                     dptokq[:, tj * 512:(tj + 1) * 512],
                                     start=(tj == 0), stop=(tj == 3))
                    return r
                tkw = ch.step("tensor", f_w1, after=None)
                tkw = ch.step("vector", lambda v, mc=mc: v.tensor_scalar_mul(
                    ptile[:, 0:512], ps[4 + mc % 2].ap()[:, 0:512], scal[:, 1:2]),
                    after=tkw)
                ch.step("vector", lambda v, mc=mc, db4=db4: v.scalar_tensor_tensor(
                    W1T[:, mc * 2048 + db4 * 512: mc * 2048 + (db4 + 1) * 512],
                    W1T[:, mc * 2048 + db4 * 512: mc * 2048 + (db4 + 1) * 512],
                    scal[:, 3:4], ptile[:, 0:512], ALU.mult, ALU.add), after=tkw)

        for dc in range(16):
            for mb2 in range(2):
                last = None
                for tj in range(4):
                    lt = ch.step("sync", lambda y, tj=tj, dc=dc, i=i: y.dma_start(
                        out=chkr[:, (tj % 2) * 128:(tj % 2 + 1) * 128],
                        in_=C.x3tok.ap()[tj, i * 128:(i + 1) * 128,
                                         dc * 128:(dc + 1) * 128]), dma=True)
                    last = ch.step("tensor", lambda t, tj=tj, mb2=mb2: t.matmul(
                        ps[4 + mb2].ap()[:, 0:512],
                        chkr[:, (tj % 2) * 128:(tj % 2 + 1) * 128],
                        dhptok[:, tj * 1024 + mb2 * 512: tj * 1024 + (mb2 + 1) * 512],
                        start=(tj == 0), stop=(tj == 3)), after=None)
                tkw = ch.step("vector", lambda v, mb2=mb2: v.tensor_scalar_mul(
                    ptile[:, 0:512], ps[4 + mb2].ap()[:, 0:512], scal[:, 1:2]),
                    after=last)
                ch.step("vector", lambda v, dc=dc, mb2=mb2: v.scalar_tensor_tensor(
                    W0T[:, dc * 1024 + mb2 * 512: dc * 1024 + (mb2 + 1) * 512],
                    W0T[:, dc * 1024 + mb2 * 512: dc * 1024 + (mb2 + 1) * 512],
                    scal[:, 3:4], ptile[:, 0:512], ALU.mult, ALU.add), after=tkw)

        def f_b(v):
            v.tensor_scalar_mul(scal[:, 4:12], scal[:, 4:12], scal[:, 1:2])
            v.scalar_tensor_tensor(b0c[:, 0:8], b0c[:, 0:8], scal[:, 3:4],
                                   scal[:, 4:12], ALU.mult, ALU.add)
            v.tensor_scalar_mul(db1c[:, 0:16], db1c[:, 0:16], scal[:, 1:2])
            return v.scalar_tensor_tensor(b1c[:, 0:16], b1c[:, 0:16], scal[:, 3:4],
                                          db1c[:, 0:16], ALU.mult, ALU.add)
        ch.step("vector", f_b, after=None)

        upd_w1tb()

        def f_chp(g, semh, i=i):
            def p_body(p):
                g.dma_start(
                    out=chp[:, :].rearrange("p (c t) -> p c t", t=128),
                    in_=C.x3loc.ap()[p][:, i * 128:(i + 1) * 128].rearrange(
                        "(c p) t -> p c t", p=128)).then_inc(semh, 16)
            pbranch(g, C.rp, P.lbl("bchp"), p_body)
        ch.step("gpsimd", f_chp, after=None, dma=True, inc=16, manual=True)

        for mc in range(NMC):
            def f_hn(t, mc=mc):
                r = None
                for dc in range(16):
                    r = t.matmul(ps[mc % 2].ap()[:, 0:128],
                                 bfv(W0T[:, dc * 1024 + mc * 128: dc * 1024 + (mc + 1) * 128]),
                                 chp[:, dc * 128:(dc + 1) * 128],
                                 start=(dc == 0), stop=(dc == 15))
                return r
            tkh = ch.step("tensor", f_hn, after=None)
            tkh = ch.step("vector", lambda v, mc=mc: v.tensor_scalar_add(
                uscr[:, 0:128], ps[mc % 2].ap()[:, 0:128],
                b0c[:, mc:mc + 1]), after=tkh)
            ch.step("scalar", lambda s, mc=mc: s.activation(
                hb[:, mc * 512: mc * 512 + 128], uscr[:, 0:128],
                AF.Silu), after=tkh)
        for dc in range(16):
            def f_o(t, dc=dc):
                r = None
                for mc in range(NMC):
                    r = t.matmul(ps[2 + dc % 2].ap()[:, 0:128],
                                 bfv(W1T[:, mc * 2048 + dc * 128: mc * 2048 + (dc + 1) * 128]),
                                 hb[:, mc * 512: mc * 512 + 128],
                                 start=(mc == 0), stop=(mc == NMC - 1))
                return r
            tko = ch.step("tensor", f_o, after=None)
            tko = ch.step("vector", lambda v, dc=dc: v.tensor_copy(
                ptile[:, 0:128], ps[2 + dc % 2].ap()[:, 0:128]), after=tko)
            ch.step("sync", lambda y, dc=dc, i=i: y.dma_start(
                out=C.aoutp.ap()[dc * 128:(dc + 1) * 128, i * 128:(i + 1) * 128],
                in_=ptile[:, 0:128]), after=tko, dma=True)

    P.release(m_step)

    def f_pub2(g, semh):
        def side_body(side):
            g.dma_start(
                out=C.shex2.ap()[side],
                in_=C.aoutp.ap()).then_inc(semh, 16)
        jbranch(g, C.rj, P.lbl("bp2"), side_body)
    ch.step("gpsimd", f_pub2, after=None, dma=True, inc=16, manual=True)
    emit_poll_and_flag(C, ch, 50, "bout")

    ldb_a = P.alloc_f32(2048, "blda")
    ldb_b = P.alloc_f32(2048, "bldb2")
    ldb_c = P.alloc_f32(2048, "bldc")
    x2flat = C.x2all.ap().rearrange("b d t -> (b d) t")
    for dc in range(16):
        def f_rd2(g, semh, dc=dc):
            def side_body(side):
                g.dma_start(
                    out=ldb_a[:, 0:2048],
                    in_=C.shex2.ap()[1 - side][dc * 128:(dc + 1) * 128, :]
                ).then_inc(semh, 16)
            jbranch(g, C.rj, P.lbl("bo"), side_body)
        t1 = ch.step("gpsimd", f_rd2, after=None, dma=True, inc=16, manual=True)
        t2 = ch.step("sync", lambda y, dc=dc: y.dma_start(
            out=ldb_b[:, 0:2048],
            in_=C.aoutp.ap()[dc * 128:(dc + 1) * 128, :]), dma=True)

        def f_x2(g, semh, dc=dc):
            def p_body(p):
                g.dma_start(
                    out=ldb_c[:, 0:2048],
                    in_=x2flat[p * 2048 + dc * 128: p * 2048 + (dc + 1) * 128, :]
                ).then_inc(semh, 16)
            pbranch(g, C.rp, P.lbl("bx2"), p_body)
        t3 = ch.step("gpsimd", f_x2, after=None, dma=True, inc=16, manual=True)

        def f_sum(v, dc=dc):
            v.tensor_add(ldb_a[:, 0:2048], ldb_a[:, 0:2048], ldb_b[:, 0:2048])
            return v.scalar_tensor_tensor(ldb_a[:, 0:2048], ldb_a[:, 0:2048],
                                          b1c[:, dc:dc + 1], ldb_c[:, 0:2048],
                                          ALU.add, ALU.add)
        t4 = ch.step("vector", f_sum, after=ch.join(t1, t2, t3))
        ch.step("sync", lambda y, dc=dc: y.dma_start(
            out=C.xffn2d.ap()[dc * 128:(dc + 1) * 128, :], in_=ldb_a[:, 0:2048]),
            after=t4, dma=True)
    if C.dout("attnb_x3n", (D, T)) is not None:
        ch.step("sync", lambda y: y.dma_start(out=C.dbg["attnb_x3n"],
                                              in_=C.xffn2d.ap()), dma=True)
    P.release(m0)



def _constants():
    identf = np.eye(128, dtype=np.float32)
    tril = np.zeros((128, 128), dtype=np.float32)
    for blk in range(2):
        o = blk * 64
        for tp in range(64):
            tril[o + tp, o + tp:o + 64] = 1.0
    suf = np.zeros((128, 128), dtype=np.float32)
    for blk in range(8):
        o = blk * 16
        for tp in range(16):
            suf[o + tp, o:o + tp + 1] = 1.0
    cm = np.zeros((128, 128), dtype=np.float32)
    for i in range(128):
        cm[i, i + 1:] = -1e30
    return identf, tril, suf, cm


_CACHED = {}


def _get_program():
    key = (STOP, tuple(DBG))
    if key not in _CACHED:
        _CACHED[key] = build_program()
    return _CACHED[key]


def build_inmaps(inputs, nonce=None):
    identf, tril, suf, cm = _constants()
    xfull = np.ascontiguousarray(inputs["L0_attn_a_hidden_states"], dtype=np.float32)

    base = {
        "aWq": inputs["L0_attn_a_Wq"], "aWk": inputs["L0_attn_a_Wk"],
        "aWv": inputs["L0_attn_a_Wv"], "aWg": inputs["L0_attn_a_Wg"],
        "aWgk1": inputs["L0_attn_a_Wgk1"], "aWgk2": inputs["L0_attn_a_Wgk2"],
        "a_bgk2": np.asarray(inputs["L0_attn_a_bgk2"]).reshape(1, -1),
        "aWo": inputs["L0_attn_a_Wo"],
        "a_lnw": np.asarray(inputs["L0_attn_a_ln_w"]).reshape(1, -1),
        "a_lnb": np.asarray(inputs["L0_attn_a_ln_b"]).reshape(1, -1),
        "fWq": inputs["L0_ffn_Wq"], "fWk": inputs["L0_ffn_Wk"],
        "fWv": inputs["L0_ffn_Wv"], "fWo": inputs["L0_ffn_Wo"],
        "fA": inputs["L0_ffn_A"], "fB": inputs["L0_ffn_B"],
        "fWin": inputs["L0_ffn_W_in_base"], "fWup": inputs["L0_ffn_W_up_base"],
        "fWdown": inputs["L0_ffn_W_down"],
        "f_ln1": np.asarray(inputs["L0_ffn_ln1_w"]).reshape(1, -1),
        "f_ln2": np.asarray(inputs["L0_ffn_ln2_w"]).reshape(1, -1),
        "b_b1": np.asarray(inputs["L0_attn_b_b1"]).reshape(1, -1),
        "b_lrw": np.asarray(inputs["L0_attn_b_lr_w"]).reshape(1, -1),
        "b_lrb": np.asarray(inputs["L0_attn_b_lr_b"]).reshape(1, 1),
        "b_fgw": np.asarray(inputs["L0_attn_b_fg_w"]).reshape(1, -1),
        "b_fgb": np.asarray(inputs["L0_attn_b_fg_b"]).reshape(1, 1),
        "identf": identf, "tril64": tril, "suf16": suf, "cmask": cm,
    }
    base = {k: np.ascontiguousarray(np.asarray(v), dtype=np.float32) for k, v in base.items()}

    W0 = np.ascontiguousarray(np.asarray(inputs["L0_attn_b_W0"], dtype=np.float32))
    W1 = np.ascontiguousarray(np.asarray(inputs["L0_attn_b_W1"], dtype=np.float32))
    b0 = np.asarray(inputs["L0_attn_b_b0"], dtype=np.float32).reshape(1, -1)

    if nonce is None:
        nonce = int(np.random.default_rng().integers(1, 2**31 - 1000))
    in_maps = []
    for c in range(8):
        J = c % 2
        pmine = c // 2
        mybatches = (2 * J, 2 * J + 1)
        m = dict(base)
        m["xin"] = np.ascontiguousarray(xfull[list(mybatches)])
        m["bW0h"] = np.ascontiguousarray(W0[J * MH:(J + 1) * MH, :])
        m["bW1h"] = np.ascontiguousarray(W1[:, J * MH:(J + 1) * MH])
        m["b_b0h"] = np.ascontiguousarray(b0[:, J * MH:(J + 1) * MH])
        meta = [J, nonce, pmine] + [0] * 13
        m["meta"] = np.array([meta], dtype=np.int32)
        in_maps.append(m)
    return in_maps


def kernel(**inputs):
    nc, P, C = _get_program()
    P_built = getattr(nc, "_built_flag", False)
    if not P_built:
        P.build()
        nc._built_flag = True
    in_maps = build_inmaps(inputs)
    res = run_bass_kernel_spmd(nc, in_maps, core_ids=list(range(8)))
    kernel.last_results = res
    out = np.empty((B, T, D), dtype=np.float32)
    for c in range(8):
        J = c % 2
        p = c // 2
        out[p, J * (T // 2):(J + 1) * (T // 2), :] = \
            res.results[c]["out"][J * (T // 2):(J + 1) * (T // 2), :]
    return out
